# version 04
# baseline (speedup 1.0000x reference)
import contextlib
import math

import numpy as np
import ml_dtypes

import concourse.bass as bass
import concourse.mybir as mybir
from concourse.bass_utils import run_bass_kernel_spmd

F32 = mybir.dt.float32
BF16 = mybir.dt.bfloat16
U32 = mybir.dt.uint32
I32 = mybir.dt.int32
AF = mybir.ActivationFunctionType
ALU = mybir.AluOpType
AX = mybir.AxisListType.X

D = 1024
T = 2048
NCORES = 8
DC = D // 128
LN_EPS = 1e-5

E0, H0 = 8, 2048
E1, C1, F1 = 32, 640, 1024
NH, DH = 16, 64
LAT = 512


class Ctr:

    def __init__(self, sem):
        self.sem = sem
        self.v = 0

    def inc(self, k=1):
        self.v += k
        return self.v


class Prog:
    ENGINES = ("tensor", "vector", "scalar", "gpsimd", "sync")

    def __init__(self, nc, en):
        self.nc = nc
        self.en = en
        self.ops = {e: [] for e in self.ENGINES}
        self.ctr = {}

    def sem(self, name):
        if name not in self.ctr:
            self.ctr[name] = Ctr(self.en(self.nc.semaphore(name)))
        return self.ctr[name]

    def add(self, eng, fn):
        gate = getattr(self, "_gate", None)
        if gate:
            def wrapped(e, fn=fn, gate=list(gate)):
                wait_all(e, gate)
                fn(e)
            self.ops[eng].append(wrapped)
        else:
            self.ops[eng].append(fn)

    def begin_stage(self, gate):
        self._gate = gate
        if getattr(self, "_arena", None) is not None:
            self._abump = 0

    def set_arena(self, tensor_bf16, nbytes):
        self._arena = tensor_bf16
        self._abytes = nbytes
        self._abump = 0

    def sbuf(self, name, shape, dtype):
        if getattr(self, "_arena", None) is None:
            return self.en(self.nc.sbuf_tensor(name, shape, dtype))
        esz = {F32: 4, BF16: 2, I32: 4, U32: 4}[dtype]
        nb = shape[1] * esz
        start = (self._abump + 63) & ~63
        self._abump = start + nb
        assert self._abump <= self._abytes,             f"arena overflow at {name}: {self._abump} > {self._abytes}"
        view = self._arena[:].bitcast(dtype) if dtype != BF16 else self._arena[:]
        div = esz // 2 if dtype != BF16 else 1
        col0 = start // esz if dtype != BF16 else start // 2
        ap = view[0:shape[0], col0:col0 + shape[1]]
        return ap

    def psum(self, name, shape, dtype=F32):
        return self.en(self.nc.psum_tensor(name, shape, dtype))

    def build_block(self):
        block = self.en(self.nc.Block())

        @block.tensor
        def _(t):
            for f in self.ops["tensor"]:
                f(t)

        @block.vector
        def _(v):
            for f in self.ops["vector"]:
                f(v)

        @block.scalar
        def _(s):
            for f in self.ops["scalar"]:
                f(s)

        @block.gpsimd
        def _(g):
            for f in self.ops["gpsimd"]:
                f(g)

        @block.sync
        def _(s):
            for f in self.ops["sync"]:
                f(s)


def wait_all(eng, waits):
    for ctr, val in waits:
        eng.wait_ge(ctr.sem, val)



def emit_load_xt(p, xt, x_dram):
    c = p.sem("ldx")
    done = c.inc(16 * DC)

    def f(sy):
        for i in range(DC):
            sy.dma_start(xt[i][:], x_dram[i * 128:(i + 1) * 128, :]).then_inc(c.sem, 16)
    p.add("sync", f)
    return [(c, done)]


def emit_store_xt(p, x_dram, xt, waits):
    c = p.sem("stx")
    done = c.inc(16 * DC)

    def f(sy):
        wait_all(sy, waits)
        for i in range(DC):
            sy.dma_start(x_dram[i * 128:(i + 1) * 128, :], xt[i][:]).then_inc(c.sem, 16)
        sy.wait_ge(c.sem, done)
    p.add("sync", f)
    return [(c, done)]


def emit_const_ones(p):
    ones = p.sbuf("c_ones", [128, 128], BF16)
    gp = p.sem("gp_const")
    done = gp.inc()

    def f(g):
        g.memset(ones[:], 1.0).then_inc(gp.sem, 1)
    p.add("gpsimd", f)
    return ones, [(gp.sem and gp, done)]


def emit_ln(p, tag, xt, xln, x_waits, gamma=None, beta=None, ones=None,
           ones_waits=(), ps=None, scr=None):
    NTB = T // 512
    scr_q = scr[:, 0:DC * 512]
    scr_b = scr[:, DC * 512:2 * DC * 512]
    pS = ps[:, 0:512]
    pQ = ps[:, 512:1024]
    _fs = scr[:, 2 * DC * 512:2 * DC * 512 + 4096].bitcast(F32)
    mu = _fs[:, 0:512]
    msq = _fs[:, 512:1024]
    var = msq
    tmp = _fs[:, 1024:1536]
    rstd = tmp
    mur = _fs[:, 1536:2048]
    u = mu

    sc = p.sem(tag + "_sc")
    mm = p.sem(tag + "_mm")
    ve = p.sem(tag + "_ve")

    mm_sum = [mm.inc() for _ in range(NTB)]
    ve_var = []
    ve_norm = []
    for tb in range(NTB):
        ve_var.append(ve.inc())
        ve_norm.append(ve.inc())
    sc_sq = [0] * NTB
    sc_rstd = [0] * NTB
    sc_sq[0] = sc.inc()
    for tb in range(1, NTB):
        sc_sq[tb] = sc.inc()
        sc_rstd[tb - 1] = sc.inc()
    sc_rstd[NTB - 1] = sc.inc()

    def s_fn(s):
        wait_all(s, x_waits)

        def do_sq(s, tb):
            sl = slice(tb * 512, (tb + 1) * 512)
            if tb > 0:
                s.wait_ge(mm.sem, mm_sum[tb - 1])
            for i in range(DC):
                csl = slice(i * 512, (i + 1) * 512)
                s.activation(scr_q[:, csl], xt[i][:, sl], AF.Square)
                s.activation(scr_b[:, csl], xt[i][:, sl], AF.Copy)
            s.drain().then_inc(sc.sem, 1)

        def do_rstd(s, tb):
            s.wait_ge(ve.sem, ve_var[tb])
            s.activation(rstd, var, AF.Ln, bias=p.cb[:, 0:1])
            s.drain()
            s.activation(rstd, rstd, AF.Exp, scale=-0.5)
            s.drain().then_inc(sc.sem, 1)

        do_sq(s, 0)
        for tb in range(1, NTB):
            do_sq(s, tb)
            do_rstd(s, tb - 1)
        do_rstd(s, NTB - 1)
    p.add("scalar", s_fn)

    def t_fn(t):
        wait_all(t, ones_waits)
        for tb in range(NTB):
            t.wait_ge(sc.sem, sc_sq[tb])
            if tb > 0:
                t.wait_ge(ve.sem, ve_var[tb - 1])
            for i in range(DC):
                csl = slice(i * 512, (i + 1) * 512)
                t.matmul(pS, ones[:], scr_b[:, csl],
                         start=(i == 0), stop=(i == DC - 1))
            for i in range(DC):
                csl = slice(i * 512, (i + 1) * 512)
                inst = t.matmul(pQ, ones[:], scr_q[:, csl],
                                start=(i == 0), stop=(i == DC - 1))
            inst.then_inc(mm.sem, 1)
    p.add("tensor", t_fn)

    def v_fn(v):
        for tb in range(NTB):
            sl = slice(tb * 512, (tb + 1) * 512)
            v.wait_ge(mm.sem, mm_sum[tb])
            v.tensor_scalar(mu, pS, 1.0 / D, None, op0=ALU.mult)
            v.tensor_scalar(msq, pQ, 1.0 / D, None, op0=ALU.mult)
            v.drain()
            v.tensor_tensor(tmp, mu, mu, op=ALU.mult)
            v.drain()
            v.tensor_tensor(var, msq, tmp, op=ALU.subtract)
            v.drain().then_inc(ve.sem, 1)
            v.wait_ge(sc.sem, sc_rstd[tb])
            v.tensor_tensor(mur, mu, rstd, op=ALU.mult)
            v.drain()
            for i in range(DC):
                v.tensor_tensor(u, xt[i][:, sl], rstd, op=ALU.mult)
                v.drain()
                if gamma is None:
                    v.tensor_tensor(xln[i][:, sl], u, mur, op=ALU.subtract)
                else:
                    v.tensor_tensor(u, u, mur, op=ALU.subtract)
                    v.drain()
                    v.tensor_scalar(xln[i][:, sl], u, gamma[:, i:i + 1],
                                    beta[:, i:i + 1], op0=ALU.mult, op1=ALU.add)
                v.drain()
            v.nop().then_inc(ve.sem, 1)
    p.add("vector", v_fn)

    return [(ve, ve_norm[-1])]


def emit_l0ffn(p, tag, xt, xln, xln_waits, wg_d, w1_d, w2_d, ones, ones_waits,
               ident, ident_waits, ps, scr, sel8, sel8_waits):
    NTC = T // 128
    NTB = T // 512
    wg = p.sbuf(tag + "_wg", [128, DC * 8], BF16)
    cT = p.sbuf(tag + "_cT", [E0, T], BF16)
    w1h = [p.sbuf(tag + "_w1a", [128, DC * 1024], BF16),
           p.sbuf(tag + "_w1b", [128, DC * 1024], BF16)]
    w2h = [p.sbuf(tag + "_w2a", [128, 16 * 512], BF16),
           p.sbuf(tag + "_w2b", [128, 16 * 512], BF16)]
    cbuf = p.sbuf(tag + "_cbuf", [128, 2 * NTB * 512], BF16)
    tmp2 = p.sbuf(tag + "_tmp2", [128, 1024], F32)
    r_m = p.sbuf(tag + "_rm", [128, 1], F32)
    r_nm = p.sbuf(tag + "_rnm", [128, 1], F32)
    r_e = p.sbuf(tag + "_re", [128, E0], F32)
    r_s = p.sbuf(tag + "_rs", [128, 1], F32)
    r_oh = p.sbuf(tag + "_roh", [128, E0], F32)
    r_lm = p.sbuf(tag + "_rlm", [128, E0], F32)
    r_e2 = p.sbuf(tag + "_re2", [128, 1], F32)
    r_d = p.sbuf(tag + "_rd", [128, 1], F32)
    r_id = p.sbuf(tag + "_rid", [128, 1], F32)
    r_c = p.sbuf(tag + "_rc", [128, E0], F32)
    r_cb = p.sbuf(tag + "_rcb", [128, E0], BF16)

    ldg = p.sem(tag + "_ldg")
    lds = p.sem(tag + "_lds")
    ldv = p.sem(tag + "_ldv")
    mm = p.sem(tag + "_mm")
    ve = p.sem(tag + "_ve")
    sc = p.sem(tag + "_sc")

    def h2ad(tb, mh):
        if mh < 8:
            return (tb % 2) * 4096 + mh * 512
        return 8192 + (mh - 8) * 512

    ldg_wg = ldg.inc(16)
    ldg_h = [[ldg.inc(16), ldg.inc(16)] for _ in range(E0)]
    lds_a = [lds.inc(16) for _ in range(E0)]
    ldv_b = [ldv.inc(16) for _ in range(E0)]
    mm_log, mm_ctr = [], []
    for tc in range(NTC):
        mm_log.append(mm.inc())
        mm_ctr.append(mm.inc())
    sc_exp, sc_csq = [], []
    for tc in range(NTC):
        sc_exp.append(sc.inc())
        sc_csq.append(sc.inc())
    ve_pre, ve_cmb, ve_csqT = [], [], []
    for tc in range(NTC):
        ve_pre.append(ve.inc())
        ve_cmb.append(ve.inc())
        ve_csqT.append(ve.inc())
    slots = [(e, tb) for e in range(E0) for tb in range(NTB)]
    mm_bc = [0] * E0
    mm_h = [[[0] * 16 for _ in range(NTB)] for _ in range(E0)]
    mm_w2a = [[[0] * 4 for _ in range(NTB)] for _ in range(E0)]
    mm_w2b = [[[0] * 4 for _ in range(NTB)] for _ in range(E0)]
    mm_bc[0] = mm.inc()
    prev = None
    for (e, tb) in slots:
        if tb == 0 and e > 0:
            mm_bc[e] = mm.inc()
        for mh in range(8):
            mm_h[e][tb][mh] = mm.inc()
        if prev is not None:
            pe, ptb = prev
            for dj in range(4):
                mm_w2b[pe][ptb][dj] = mm.inc()
        for mh in range(8, 16):
            mm_h[e][tb][mh] = mm.inc()
        for dj in range(4):
            mm_w2a[e][tb][dj] = mm.inc()
        prev = (e, tb)
    for dj in range(4):
        mm_w2b[E0 - 1][NTB - 1][dj] = mm.inc()
    sc_h = [[[0] * 16 for _ in range(NTB)] for _ in range(E0)]
    for (e, tb) in slots:
        for mh in range(16):
            sc_h[e][tb][mh] = sc.inc()
    ve_bc = [0] * E0
    ve_sq = [[[0] * 16 for _ in range(NTB)] for _ in range(E0)]
    ve_evA = [[0] * NTB for _ in range(E0)]
    ve_evB = [[0] * NTB for _ in range(E0)]
    ve_bc[0] = ve.inc()
    prev = None
    for (e, tb) in slots:
        if tb == 0 and e > 0:
            ve_bc[e] = ve.inc()
        for mh in range(8):
            ve_sq[e][tb][mh] = ve.inc()
        if prev is not None:
            pe, ptb = prev
            ve_evB[pe][ptb] = ve.inc()
        for mh in range(8, 16):
            ve_sq[e][tb][mh] = ve.inc()
        ve_evA[e][tb] = ve.inc()
        prev = (e, tb)
    ve_evB[E0 - 1][NTB - 1] = ve.inc()

    def g_fn(g):
        for i in range(DC):
            g.dma_start(wg[:, i * 8:(i + 1) * 8],
                        wg_d[i * 128:(i + 1) * 128, :]).then_inc(ldg.sem, 16)
        for e in range(E0):
            if e > 0:
                g.wait_ge(mm.sem, mm_h[e - 1][NTB - 1][7])
            g.dma_start(w1h[0][:],
                        w1_d[e][:, 0:1024].rearrange("(c p) h -> p c h", p=128)
                        ).then_inc(ldg.sem, 16)
            if e > 0:
                g.wait_ge(mm.sem, mm_h[e - 1][NTB - 1][15])
            g.dma_start(w1h[1][:],
                        w1_d[e][:, 1024:2048].rearrange("(c p) h -> p c h", p=128)
                        ).then_inc(ldg.sem, 16)
            if e > 0:
                g.wait_ge(mm.sem, mm_w2a[e - 1][NTB - 1][3])
            g.dma_start(w2h[0][:],
                        w2_d[e][:, 0:512].rearrange("(m p) d -> p m d", p=128)
                        ).then_inc(lds.sem, 16)
            if e > 0:
                g.wait_ge(mm.sem, mm_w2b[e - 1][NTB - 1][3])
            g.dma_start(w2h[1][:],
                        w2_d[e][:, 512:1024].rearrange("(m p) d -> p m d", p=128)
                        ).then_inc(ldv.sem, 16)
    p.add("gpsimd", g_fn)

    def t_fn(t):
        wait_all(t, xln_waits)
        wait_all(t, ones_waits)
        wait_all(t, ident_waits)
        wait_all(t, sel8_waits)
        t.wait_ge(ldg.sem, ldg_wg)
        ps_bf = ps[:].bitcast(BF16)
        for tc in range(NTC):
            tsl = slice(tc * 128, (tc + 1) * 128)
            if tc > 0:
                t.wait_ge(ve.sem, ve_cmb[tc - 1])
            for i in range(DC):
                inst = t.matmul(ps[:, 0:8], xln[i][:, tsl],
                                wg[:, i * 8:(i + 1) * 8],
                                start=(i == 0), stop=(i == DC - 1))
            inst.then_inc(mm.sem, 1)
            t.wait_ge(sc.sem, sc_csq[tc])
            if tc > 0:
                t.wait_ge(ve.sem, ve_csqT[tc - 1])
            t.transpose(ps_bf[0:E0, 1024:1024 + 128], r_cb[:],
                        ident[:]).then_inc(mm.sem, 1)
        t.wait_ge(ve.sem, ve_csqT[NTC - 1])
        prev = None
        for (e, tb) in slots:
            tbsl = slice(tb * 512, (tb + 1) * 512)
            if tb == 0:
                if e > 0:
                    t.wait_ge(sc.sem, sc_h[e - 1][NTB - 1][15])
                for q in range(NTB):
                    inst = t.matmul(
                        ps[:, 2048 + q * 512:2048 + (q + 1) * 512],
                        sel8[0:E0, e * 128:(e + 1) * 128],
                        cT[0:E0, q * 512:(q + 1) * 512],
                        start=True, stop=True)
                inst.then_inc(mm.sem, 1)
            for mh in range(16):
                if mh == 0:
                    if tb == 0:
                        t.wait_ge(ve.sem, ve_bc[e])
                        t.wait_ge(ldg.sem, ldg_h[e][0])
                    else:
                        t.wait_ge(sc.sem, sc_h[e][tb - 1][15])
                if mh == 8:
                    if prev is not None:
                        pe, ptb = prev
                        t.wait_ge(ve.sem, ve_evA[pe][ptb])
                        if ptb == 0:
                            t.wait_ge(ldv.sem, ldv_b[pe])
                        for dj in range(4):
                            for kmh in range(16):
                                inst = t.matmul(
                                    ps[:, dj * 512:(dj + 1) * 512],
                                    w2h[1][:, kmh * 512 + dj * 128:
                                            kmh * 512 + (dj + 1) * 128],
                                    scr[:, h2ad(ptb, kmh):h2ad(ptb, kmh) + 512],
                                    start=(kmh == 0), stop=(kmh == 15))
                            inst.then_inc(mm.sem, 1)
                    if tb == 0:
                        t.wait_ge(ldg.sem, ldg_h[e][1])
                if mh >= 4:
                    t.wait_ge(sc.sem, sc_h[e][tb][mh - 4])
                bank = 2048 + (mh % 4) * 512
                wh = w1h[0] if mh < 8 else w1h[1]
                for c in range(DC):
                    inst = t.matmul(
                        ps[:, bank:bank + 512],
                        wh[:, c * 1024 + (mh % 8) * 128:
                           c * 1024 + (mh % 8) * 128 + 128],
                        xln[c][:, tbsl],
                        start=(c == 0), stop=(c == DC - 1))
                inst.then_inc(mm.sem, 1)
            if prev is not None:
                pe, ptb = prev
                t.wait_ge(ve.sem, ve_evB[pe][ptb])
            if tb == 0:
                t.wait_ge(lds.sem, lds_a[e])
            for dj in range(4):
                for mh in range(16):
                    if dj == 0:
                        t.wait_ge(ve.sem, ve_sq[e][tb][mh])
                    inst = t.matmul(
                        ps[:, dj * 512:(dj + 1) * 512],
                        w2h[0][:, mh * 512 + dj * 128:
                                mh * 512 + (dj + 1) * 128],
                        scr[:, h2ad(tb, mh):h2ad(tb, mh) + 512],
                        start=(mh == 0), stop=(mh == 15))
                inst.then_inc(mm.sem, 1)
            prev = (e, tb)
        pe, ptb = prev
        t.wait_ge(ve.sem, ve_evA[pe][ptb])
        for dj in range(4):
            for mh in range(16):
                inst = t.matmul(
                    ps[:, dj * 512:(dj + 1) * 512],
                    w2h[1][:, mh * 512 + dj * 128:mh * 512 + (dj + 1) * 128],
                    scr[:, h2ad(ptb, mh):h2ad(ptb, mh) + 512],
                    start=(mh == 0), stop=(mh == 15))
            inst.then_inc(mm.sem, 1)
    p.add("tensor", t_fn)

    def s_fn(s):
        for tc in range(NTC):
            s.wait_ge(ve.sem, ve_pre[tc])
            s.activation(r_e[:], ps[:, 0:8], AF.Exp,
                         bias=r_nm[:, 0:1], accum_out=r_s[:])
            s.drain().then_inc(sc.sem, 1)
            s.wait_ge(ve.sem, ve_cmb[tc])
            if tc > 0:
                s.wait_ge(mm.sem, mm_ctr[tc - 1])
            s.activation(r_cb[:], r_c[:], AF.Copy, scale=0.5)
            s.drain().then_inc(sc.sem, 1)
        prev = None
        for (e, tb) in slots:
            for mh in range(16):
                if mh == 8 and prev is not None:
                    pe, ptb = prev
                    s.wait_ge(mm.sem, mm_w2b[pe][ptb][3])
                s.wait_ge(mm.sem, mm_h[e][tb][mh])
                bank = 2048 + (mh % 4) * 512
                s.activation(scr[:, h2ad(tb, mh):h2ad(tb, mh) + 512],
                             ps[:, bank:bank + 512], AF.Relu)
                s.drain().then_inc(sc.sem, 1)
            prev = (e, tb)
    p.add("scalar", s_fn)

    def v_fn(v):
        ps_bf = ps[:].bitcast(BF16)
        for tc in range(NTC):
            tsl = slice(tc * 128, (tc + 1) * 128)
            v.wait_ge(mm.sem, mm_log[tc])
            v.reduce_max(r_m[:], ps[:, 0:8], axis=AX)
            v.drain()
            v.tensor_scalar(r_nm[:], r_m[:], -1.0, None, op0=ALU.mult)
            v.drain().then_inc(ve.sem, 1)
            v.wait_ge(sc.sem, sc_exp[tc])
            v.tensor_scalar(r_oh[:], ps[:, 0:8], r_m[:, 0:1], None,
                            op0=ALU.is_equal)
            v.drain()
            v.scalar_tensor_tensor(r_lm[:], r_oh[:], -1.0, r_e[:],
                                   op0=ALU.mult, op1=ALU.add)
            v.drain()
            v.reduce_max(r_e2[:], r_lm[:], axis=AX)
            v.drain()
            v.scalar_tensor_tensor(r_d[:], r_s[:], 1e-8, r_e2[:],
                                   op0=ALU.mult, op1=ALU.add)
            v.drain()
            v.tensor_scalar(r_d[:], r_d[:], 1.0, None, op0=ALU.add)
            v.drain()
            v.reciprocal(r_id[:], r_d[:])
            v.drain()
            v.tensor_scalar(r_c[:], r_e[:], r_e2[:, 0:1], None, op0=ALU.is_ge)
            v.drain()
            v.tensor_tensor(r_c[:], r_c[:], r_e[:], op=ALU.mult)
            v.drain()
            v.tensor_scalar(r_c[:], r_c[:], r_id[:, 0:1], None, op0=ALU.mult)
            v.drain().then_inc(ve.sem, 1)
            v.wait_ge(mm.sem, mm_ctr[tc])
            v.tensor_copy(cT[:, tsl], ps_bf[0:E0, 1024:1024 + 128])
            v.drain().then_inc(ve.sem, 1)

        def bc_evac(e):
            v.wait_ge(mm.sem, mm_bc[e])
            par = e % 2
            for q in range(NTB):
                v.tensor_copy(cbuf[:, par * 2048 + q * 512:
                                   par * 2048 + (q + 1) * 512],
                              ps[:, 2048 + q * 512:2048 + (q + 1) * 512])
            v.drain().then_inc(ve.sem, 1)

        def w2_evac(e, tb, half, mms):
            cB = cbuf[:, (e % 2) * 2048 + tb * 512:
                      (e % 2) * 2048 + (tb + 1) * 512]
            base = 0 if half == 0 else 4
            for pairk in range(2):
                for u in range(2):
                    dj = 2 * pairk + u
                    v.wait_ge(mm.sem, mms[dj])
                    v.tensor_tensor(tmp2[:, u * 512:(u + 1) * 512],
                                    ps[:, dj * 512:(dj + 1) * 512], cB,
                                    op=ALU.mult)
                v.drain()
                for u in range(2):
                    dj = 2 * pairk + u
                    v.tensor_tensor(xt[base + dj][:, tb * 512:(tb + 1) * 512],
                                    xt[base + dj][:, tb * 512:(tb + 1) * 512],
                                    tmp2[:, u * 512:(u + 1) * 512], op=ALU.add)
                v.drain()
            v.nop().then_inc(ve.sem, 1)

        bc_evac(0)
        prev = None
        for (e, tb) in slots:
            if tb == 0 and e > 0:
                bc_evac(e)
            for mh in range(16):
                if mh == 8 and prev is not None:
                    pe, ptb = prev
                    w2_evac(pe, ptb, 1, mm_w2b[pe][ptb])
                v.wait_ge(sc.sem, sc_h[e][tb][mh])
                ad = h2ad(tb, mh)
                v.tensor_tensor(scr[:, ad:ad + 512], scr[:, ad:ad + 512],
                                scr[:, ad:ad + 512], op=ALU.mult)
                v.drain().then_inc(ve.sem, 1)
            w2_evac(e, tb, 0, mm_w2a[e][tb])
            prev = (e, tb)
        pe, ptb = prev
        w2_evac(pe, ptb, 1, mm_w2b[pe][ptb])
    p.add("vector", v_fn)

    return [(ve, ve_evB[E0 - 1][NTB - 1])]


def emit_consts(p, ident_d, sel8_d=None):
    ones = p.sbuf("c_ones", [128, 128], BF16)
    ident = p.sbuf("c_ident", [128, 128], BF16)
    gp = p.sem("gp_const")
    cb_done = gp.inc()
    ones_done = gp.inc()
    ld = p.sem("ld_const")
    id_done = ld.inc(16)
    sel8 = None
    if sel8_d is not None:
        sel8 = p.sbuf("c_sel8", [E0, E0 * 128], BF16)
        id_done = ld.inc(16)
    cb = p.sbuf("c_bank", [128, 2], F32)

    def f(g):
        g.memset(cb[:, 0:1], LN_EPS)
        g.memset(cb[:, 1:2], -4.0)
        g.drain().then_inc(gp.sem, 1)
        g.memset(ones[:], 1.0).then_inc(gp.sem, 1)
        g.dma_start(ident[:], ident_d).then_inc(ld.sem, 16)
        if sel8_d is not None:
            g.dma_start(sel8[:], sel8_d).then_inc(ld.sem, 16)
    p.add("gpsimd", f)
    p.cb = cb
    return ones, [(gp, ones_done)], ident, [(ld, id_done)], sel8



def build_l0ffn_launch():
    nc = bass.Bass("TRN2", target_bir_lowering=False)
    x_d = nc.dram_tensor("x", [D, T], F32, kind="ExternalInput").ap()
    wg_d = nc.dram_tensor("wg", [D, E0], F32, kind="ExternalInput").ap()
    w1_d = nc.dram_tensor("w1", [E0, D, H0], F32, kind="ExternalInput").ap()
    w2_d = nc.dram_tensor("w2", [E0, H0, D], F32, kind="ExternalInput").ap()
    ident_d = nc.dram_tensor("ident", [128, 128], F32, kind="ExternalInput").ap()
    sel8_d = nc.dram_tensor("sel8", [E0, E0 * 128], F32, kind="ExternalInput").ap()
    xo_d = nc.dram_tensor("xo", [D, T], F32, kind="ExternalOutput").ap()
    with contextlib.ExitStack() as stack:
        p = Prog(nc, stack.enter_context)
        xt = [p.sbuf(f"xt{i}", [128, T], F32) for i in range(DC)]
        xln = [p.sbuf(f"xln{i}", [128, T], BF16) for i in range(DC)]
        ps = p.psum("ps", [128, 4096])
        scr_t = p.sbuf("scr", [128, 24 * 512], BF16)
        scr = scr_t[:]
        ones, ones_w, ident, ident_w, sel8 = emit_consts(p, ident_d, sel8_d)
        xw = emit_load_xt(p, xt, x_d)
        lnw = emit_ln(p, "ln", xt, xln, xw, ones=ones,
                      ones_waits=ones_w + ident_w, ps=ps, scr=scr)
        outw = emit_l0ffn(p, "f0", xt, xln, lnw, wg_d, w1_d, w2_d,
                          ones, ones_w, ident, ident_w, ps, scr,
                          sel8, ident_w)
        emit_store_xt(p, xo_d, xt, outw)
        p.build_block()
    return nc


def emit_l0attn(p, tag, xt, xln, xln_waits, wd, ones, ones_waits, ps, scr,
                ln_scr_free, ident=None, ident_waits=()):
    NTB = T // 512
    NTC = T // 128
    KMAX = [4 * qb + 4 for qb in range(NTB)]
    wuqT = p.sbuf(tag + "_wuqT", [128, 4 * 1024], BF16)
    wdqT = p.sbuf(tag + "_wdqT", [128, DC * 512], BF16)
    wdkvT = p.sbuf(tag + "_wdkvT", [128, DC * 512], BF16)
    keff = p.sbuf(tag + "_keff", [128, DC * 512], BF16)
    veff = p.sbuf(tag + "_veff", [128, 4 * 1024], BF16)
    ckvT = p.sbuf(tag + "_ckvT", [128, 4 * T], BF16)
    ckv = p.sbuf(tag + "_ckv", [128, NTC * 512], BF16)
    mask4 = p.sbuf(tag + "_mask4", [128, 4 * 512], BF16)
    ones32 = p.sbuf(tag + "_ones32", [1, 128], F32)
    qlat = scr[:, 0:2048]
    ylat_sb = scr[:, 2048:4096]
    Pt = [scr[:, 4096:4608], scr[:, 4608:5120]]
    _f2 = scr[:, 8192:12288].bitcast(F32)
    invsB = _f2[:, 0:512]
    t1 = _f2[:, 512:1024]
    lns = _f2[0:1, 1024:1536]
    invs = _f2[0:1, 1536:2048]
    qT = xln
    wdq_n = scr[:, 0:4096]
    BTs = scr[:, 4096:12288]
    wuk_s = scr[:, 0:4096]
    wuv_s = scr[:, 0:4096]
    woT_s = scr[:, 4096:12288]
    cqT = scr[:, 0:8192]

    ld = p.sem(tag + "_ld")
    mm = p.sem(tag + "_mm")
    ve = p.sem(tag + "_ve")
    sc = p.sem(tag + "_sc")

    ld_g1 = ld.inc(16 * 5)
    ld_g2 = ld.inc(16)
    ld_g3 = ld.inc(16 * 2)
    mm_BT = [mm.inc() for _ in range(16)]
    mm_keff = [mm.inc() for _ in range(DC)]
    mm_veff = [mm.inc() for _ in range(8)]
    mm_cq = [mm.inc() for _ in range(16)]
    mm_ckvT = [mm.inc() for _ in range(16)]
    mm_ckv = [mm.inc() for _ in range(NTC)]
    mm_qT = [mm.inc() for _ in range(32)]
    mm_ql = [[[0] * 4 for _ in range(NTB)] for _ in range(NH)]
    mm_S = [[[0] * KMAX[qb] for qb in range(NTB)] for _ in range(NH)]
    mm_pv = [[[0] * KMAX[qb] for qb in range(NTB)] for _ in range(NH)]
    mm_ib = [[0] * NTB for _ in range(NH)]
    mm_y = [[0] * NTB for _ in range(NH)]
    for h in range(NH):
        for qb in range(NTB):
            k = KMAX[qb]
            for lc in range(4):
                mm_ql[h][qb][lc] = mm.inc()
            mm_S[h][qb][0] = mm.inc()
            for kc in range(k):
                if kc + 1 < k:
                    mm_S[h][qb][kc + 1] = mm.inc()
                mm_pv[h][qb][kc] = mm.inc()
            mm_ib[h][qb] = mm.inc()
            mm_y[h][qb] = mm.inc()
    sc_p = [[[0] * KMAX[qb] for qb in range(NTB)] for _ in range(NH)]
    sc_inv = [[0] * NTB for _ in range(NH)]
    for h in range(NH):
        for qb in range(NTB):
            for kc in range(KMAX[qb]):
                sc_p[h][qb][kc] = sc.inc()
            sc_inv[h][qb] = sc.inc()
    ve_BT = [ve.inc() for _ in range(16)]
    ve_keff = [ve.inc() for _ in range(DC)]
    ve_veff = [ve.inc() for _ in range(8)]
    ve_cq = [ve.inc() for _ in range(16)]
    ve_ckvT = [ve.inc() for _ in range(16)]
    ve_ckv = [ve.inc() for _ in range(NTC)]
    ve_qT = [ve.inc() for _ in range(32)]
    ve_ql = [[[0] * 4 for _ in range(NTB)] for _ in range(NH)]
    ve_yl = [[0] * NTB for _ in range(NH)]
    ve_ib = [[0] * NTB for _ in range(NH)]
    ve_xt = [[0] * NTB for _ in range(NH)]
    for h in range(NH):
        for qb in range(NTB):
            for lc in range(4):
                ve_ql[h][qb][lc] = ve.inc()
            ve_yl[h][qb] = ve.inc()
            ve_ib[h][qb] = ve.inc()
            ve_xt[h][qb] = ve.inc()

    gp0 = p.sem(tag + "_gp")
    ones32_done = gp0.inc()

    def g_fn(g):
        g.memset(ones32[:], 1.0).then_inc(gp0.sem, 1)
        wait_all(g, ln_scr_free)
        g.dma_start(wuqT[:], wd["wuqT"].rearrange("(k p) f -> p k f", p=128)
                    ).then_inc(ld.sem, 16)
        g.dma_start(wdq_n, wd["wdq"].rearrange("(k p) f -> p k f", p=128)
                    ).then_inc(ld.sem, 16)
        g.dma_start(wdqT[:], wd["wdqT"].rearrange("(k p) f -> p k f", p=128)
                    ).then_inc(ld.sem, 16)
        g.dma_start(wdkvT[:], wd["wdkvT"].rearrange("(k p) f -> p k f", p=128)
                    ).then_inc(ld.sem, 16)
        g.dma_start(mask4[:], wd["mask4"]).then_inc(ld.sem, 16)
        g.wait_ge(mm.sem, mm_BT[15])
        g.dma_start(wuk_s, wd["wuk"].rearrange("(k p) f -> p k f", p=128)
                    ).then_inc(ld.sem, 16)
        g.wait_ge(mm.sem, mm_keff[DC - 1])
        g.dma_start(wuv_s, wd["wuv"].rearrange("(k p) f -> p k f", p=128)
                    ).then_inc(ld.sem, 16)
        g.dma_start(woT_s, wd["woT"].rearrange("(k p) f -> p k f", p=128)
                    ).then_inc(ld.sem, 16)
    p.add("gpsimd", g_fn)

    def t_fn(t):
        wait_all(t, ones_waits)
        wait_all(t, ident_waits)
        wait_all(t, xln_waits)
        t.wait_ge(ld.sem, ld_g1)
        gi = 0
        for c in range(DC):
            for dh in range(2):
                bank = (gi % 2) * 512
                if gi >= 2:
                    t.wait_ge(ve.sem, ve_BT[gi - 2])
                for kq in range(4):
                    inst = t.matmul(
                        ps[:, bank:bank + 512],
                        wuqT[:, kq * 1024 + c * 128: kq * 1024 + (c + 1) * 128],
                        wdq_n[:, kq * 1024 + dh * 512: kq * 1024 + (dh + 1) * 512],
                        start=(kq == 0), stop=(kq == 3))
                inst.then_inc(mm.sem, 1)
                gi += 1
        t.wait_ge(ld.sem, ld_g2)
        for dch in range(DC):
            bank = (dch % 2) * 512
            if dch >= 2:
                t.wait_ge(ve.sem, ve_keff[dch - 2])
            else:
                t.wait_ge(ve.sem, ve_BT[15])
            for c in range(DC):
                inst = t.matmul(
                    ps[:, bank:bank + 512],
                    BTs[:, c * 1024 + dch * 128: c * 1024 + (dch + 1) * 128],
                    wuk_s[:, c * 512:(c + 1) * 512],
                    start=(c == 0), stop=(c == DC - 1))
            inst.then_inc(mm.sem, 1)
        t.wait_ge(ld.sem, ld_g3)
        gi = 0
        for lc in range(4):
            for hh in range(2):
                bank = (gi % 2) * 512
                if gi >= 2:
                    t.wait_ge(ve.sem, ve_veff[gi - 2])
                else:
                    t.wait_ge(ve.sem, ve_keff[DC - 1])
                for c in range(DC):
                    inst = t.matmul(
                        ps[:, bank:bank + 512],
                        wuv_s[:, c * 512 + lc * 128: c * 512 + (lc + 1) * 128],
                        woT_s[:, c * 1024 + hh * 512: c * 1024 + (hh + 1) * 512],
                        start=(c == 0), stop=(c == DC - 1))
                inst.then_inc(mm.sem, 1)
                gi += 1
        t.wait_ge(ve.sem, ve_veff[7])
        gi = 0
        for qc in range(4):
            for tb in range(NTB):
                bank = (gi % 2) * 512
                if gi >= 2:
                    t.wait_ge(ve.sem, ve_cq[gi - 2])
                for c in range(DC):
                    inst = t.matmul(
                        ps[:, bank:bank + 512],
                        wdqT[:, c * 512 + qc * 128: c * 512 + (qc + 1) * 128],
                        xln[c][:, tb * 512:(tb + 1) * 512],
                        start=(c == 0), stop=(c == DC - 1))
                inst.then_inc(mm.sem, 1)
                gi += 1
        gi = 0
        for lc in range(4):
            for tb in range(NTB):
                bank = (gi % 2) * 512
                if gi >= 2:
                    t.wait_ge(ve.sem, ve_ckvT[gi - 2])
                else:
                    t.wait_ge(ve.sem, ve_cq[15])
                for c in range(DC):
                    inst = t.matmul(
                        ps[:, bank:bank + 512],
                        wdkvT[:, c * 512 + lc * 128: c * 512 + (lc + 1) * 128],
                        xln[c][:, tb * 512:(tb + 1) * 512],
                        start=(c == 0), stop=(c == DC - 1))
                inst.then_inc(mm.sem, 1)
                gi += 1
        for tc in range(NTC):
            bank = (tc % 2) * 512
            if tc >= 2:
                t.wait_ge(ve.sem, ve_ckv[tc - 2])
            else:
                t.wait_ge(ve.sem, ve_ckvT[15])
            for c in range(DC):
                inst = t.matmul(
                    ps[:, bank:bank + 512],
                    xln[c][:, tc * 128:(tc + 1) * 128],
                    wdkvT[:, c * 512:(c + 1) * 512],
                    start=(c == 0), stop=(c == DC - 1))
            inst.then_inc(mm.sem, 1)
        gi = 0
        for dchunk in range(DC):
            for tb in range(NTB):
                bank = (gi % 2) * 512
                if gi >= 2:
                    t.wait_ge(ve.sem, ve_qT[gi - 2])
                else:
                    t.wait_ge(ve.sem, ve_ckv[NTC - 1])
                for kq in range(4):
                    inst = t.matmul(
                        ps[:, bank:bank + 512],
                        wuqT[:, kq * 1024 + dchunk * 128: kq * 1024 + (dchunk + 1) * 128],
                        cqT[:, kq * 2048 + tb * 512: kq * 2048 + (tb + 1) * 512],
                        start=(kq == 0), stop=(kq == 3))
                inst.then_inc(mm.sem, 1)
                gi += 1
        SB = [0, 512]
        YL = 1024
        SS = 3072
        YB = 3584
        for h in range(NH):
            ch, base = h // 2, (h % 2) * 64
            for qb in range(NTB):
                k = KMAX[qb]
                qsl = slice(qb * 512, (qb + 1) * 512)
                for lc in range(4):
                    bank = SB[lc % 2]
                    if h == 0 and qb == 0:
                        if lc < 2:
                            t.wait_ge(ve.sem, ve_qT[31])
                    elif lc < 2:
                        ph, pqb = (h, qb - 1) if qb > 0 else (h - 1, NTB - 1)
                        t.wait_ge(sc.sem, sc_p[ph][pqb][KMAX[pqb] - 1])
                    if lc >= 2:
                        t.wait_ge(ve.sem, ve_ql[h][qb][lc - 2])
                    t.matmul(ps[:, bank:bank + 512],
                             keff[base:base + 64,
                                  ch * 512 + lc * 128: ch * 512 + (lc + 1) * 128],
                             qT[ch][base:base + 64, qsl],
                             start=True, stop=True).then_inc(mm.sem, 1)
                t.wait_ge(ve.sem, ve_ql[h][qb][3])
                def s_mm(t, kc):
                    bank = SB[kc % 2]
                    diag = 4 * qb <= kc
                    for lc in range(4):
                        inst = t.matmul(
                            ps[:, bank:bank + 512],
                            ckvT[:, lc * T + kc * 128: lc * T + (kc + 1) * 128],
                            qlat[:, lc * 512:(lc + 1) * 512],
                            start=(lc == 0), stop=(lc == 3 and not diag))
                    if diag:
                        r = kc - 4 * qb
                        inst = t.matmul(ps[:, bank:bank + 512], ident[:],
                                        mask4[:, r * 512:(r + 1) * 512],
                                        start=False, stop=True)
                    inst.then_inc(mm.sem, 1)
                s_mm(t, 0)
                for kc in range(k):
                    if kc + 1 < k:
                        if kc >= 1:
                            t.wait_ge(sc.sem, sc_p[h][qb][kc - 1])
                        s_mm(t, kc + 1)
                    t.wait_ge(sc.sem, sc_p[h][qb][kc])
                    for lc in range(4):
                        t.matmul(ps[:, YL + lc * 512: YL + (lc + 1) * 512],
                                 ckv[:, kc * 512 + lc * 128: kc * 512 + (lc + 1) * 128],
                                 Pt[kc % 2],
                                 start=(kc == 0), stop=(kc == k - 1))
                    t.matmul(ps[0:1, SS:SS + 512], ones[:, 0:1], Pt[kc % 2],
                             start=(kc == 0),
                             stop=(kc == k - 1)).then_inc(mm.sem, 1)
                t.wait_ge(sc.sem, sc_inv[h][qb])
                if h == 0 and qb == 0:
                    t.wait_ge(gp0.sem, ones32_done)
                t.matmul(ps[:, SS:SS + 512], ones32[:], invs,
                         start=True, stop=True).then_inc(mm.sem, 1)
                t.wait_ge(ve.sem, ve_yl[h][qb])
                if not (h == 0 and qb == 0):
                    ph, pqb = (h, qb - 1) if qb > 0 else (h - 1, NTB - 1)
                    t.wait_ge(ve.sem, ve_xt[ph][pqb])
                for lc in range(4):
                    inst = t.matmul(
                        ps[base:base + 64, YB:YB + 512],
                        veff[:, lc * 1024 + h * 64: lc * 1024 + (h + 1) * 64],
                        ylat_sb[:, lc * 512:(lc + 1) * 512],
                        start=(lc == 0), stop=(lc == 3))
                inst.then_inc(mm.sem, 1)
    p.add("tensor", t_fn)

    def s_fn(s):
        SB = [0, 512]
        SS = 3072
        for h in range(NH):
            for qb in range(NTB):
                k = KMAX[qb]
                for kc in range(k):
                    s.wait_ge(mm.sem, mm_S[h][qb][kc])
                    if kc >= 2:
                        s.wait_ge(mm.sem, mm_pv[h][qb][kc - 2])
                    elif not (h == 0 and qb == 0):
                        ph, pqb = (h, qb - 1) if qb > 0 else (h - 1, NTB - 1)
                        s.wait_ge(mm.sem, mm_pv[ph][pqb][KMAX[pqb] - 1])
                    s.activation(Pt[kc % 2], ps[:, SB[kc % 2]:SB[kc % 2] + 512],
                                 AF.Exp, scale=0.125, bias=p.cb[:, 1:2])
                    s.drain().then_inc(sc.sem, 1)
                s.wait_ge(mm.sem, mm_pv[h][qb][k - 1])
                s.activation(lns, ps[0:1, SS:SS + 512], AF.Ln)
                s.drain()
                s.activation(invs, lns[:], AF.Exp, scale=-1.0)
                s.drain().then_inc(sc.sem, 1)
    p.add("scalar", s_fn)

    def v_fn(v):
        SB = [0, 512]
        YL = 1024
        SS = 3072
        YB = 3584
        gi = 0
        for c in range(DC):
            for dh in range(2):
                v.wait_ge(mm.sem, mm_BT[gi])
                v.tensor_copy(BTs[:, c * 1024 + dh * 512: c * 1024 + (dh + 1) * 512],
                              ps[:, (gi % 2) * 512:(gi % 2) * 512 + 512])
                v.drain().then_inc(ve.sem, 1)
                gi += 1
        for dch in range(DC):
            v.wait_ge(mm.sem, mm_keff[dch])
            v.tensor_copy(keff[:, dch * 512:(dch + 1) * 512],
                          ps[:, (dch % 2) * 512:(dch % 2) * 512 + 512])
            v.drain().then_inc(ve.sem, 1)
        gi = 0
        for lc in range(4):
            for hh in range(2):
                v.wait_ge(mm.sem, mm_veff[gi])
                v.tensor_copy(veff[:, lc * 1024 + hh * 512: lc * 1024 + (hh + 1) * 512],
                              ps[:, (gi % 2) * 512:(gi % 2) * 512 + 512])
                v.drain().then_inc(ve.sem, 1)
                gi += 1
        gi = 0
        for qc in range(4):
            for tb in range(NTB):
                v.wait_ge(mm.sem, mm_cq[gi])
                v.tensor_copy(cqT[:, qc * 2048 + tb * 512: qc * 2048 + (tb + 1) * 512],
                              ps[:, (gi % 2) * 512:(gi % 2) * 512 + 512])
                v.drain().then_inc(ve.sem, 1)
                gi += 1
        gi = 0
        for lc in range(4):
            for tb in range(NTB):
                v.wait_ge(mm.sem, mm_ckvT[gi])
                v.tensor_copy(ckvT[:, lc * T + tb * 512: lc * T + (tb + 1) * 512],
                              ps[:, (gi % 2) * 512:(gi % 2) * 512 + 512])
                v.drain().then_inc(ve.sem, 1)
                gi += 1
        for tc in range(NTC):
            v.wait_ge(mm.sem, mm_ckv[tc])
            v.tensor_copy(ckv[:, tc * 512:(tc + 1) * 512],
                          ps[:, (tc % 2) * 512:(tc % 2) * 512 + 512])
            v.drain().then_inc(ve.sem, 1)
        gi = 0
        for dchunk in range(DC):
            for tb in range(NTB):
                v.wait_ge(mm.sem, mm_qT[gi])
                v.tensor_copy(qT[dchunk][:, tb * 512:(tb + 1) * 512],
                              ps[:, (gi % 2) * 512:(gi % 2) * 512 + 512])
                v.drain().then_inc(ve.sem, 1)
                gi += 1
        for h in range(NH):
            ch, base = h // 2, (h % 2) * 64
            for qb in range(NTB):
                k = KMAX[qb]
                qsl = slice(qb * 512, (qb + 1) * 512)
                for lc in range(4):
                    v.wait_ge(mm.sem, mm_ql[h][qb][lc])
                    v.tensor_copy(qlat[:, lc * 512:(lc + 1) * 512],
                                  ps[:, SB[lc % 2]:SB[lc % 2] + 512])
                    v.drain().then_inc(ve.sem, 1)
                v.wait_ge(mm.sem, mm_pv[h][qb][k - 1])
                for lc in range(4):
                    v.tensor_copy(ylat_sb[:, lc * 512:(lc + 1) * 512],
                                  ps[:, YL + lc * 512: YL + (lc + 1) * 512])
                v.drain().then_inc(ve.sem, 1)
                v.wait_ge(mm.sem, mm_ib[h][qb])
                v.tensor_copy(invsB, ps[:, SS:SS + 512])
                v.drain().then_inc(ve.sem, 1)
                v.wait_ge(mm.sem, mm_y[h][qb])
                v.tensor_tensor(t1[base:base + 64, :],
                                ps[base:base + 64, YB:YB + 512],
                                invsB[base:base + 64, :], op=ALU.mult)
                v.drain()
                v.tensor_tensor(xt[ch][base:base + 64, qsl],
                                xt[ch][base:base + 64, qsl],
                                t1[base:base + 64, :], op=ALU.add)
                v.drain().then_inc(ve.sem, 1)
    p.add("vector", v_fn)

    return [(ve, ve_xt[NH - 1][NTB - 1])]


def build_l0attn_launch():
    nc = bass.Bass("TRN2", target_bir_lowering=False)
    x_d = nc.dram_tensor("x", [D, T], F32, kind="ExternalInput").ap()
    names = ["wuqT", "wdq", "wdqT", "wdkvT", "wuk", "wuv", "woT", "mask4"]
    shapes = {"wuqT": [512, 1024], "wdq": [512, 1024], "wdqT": [1024, 512],
              "wdkvT": [1024, 512], "wuk": [1024, 512], "wuv": [1024, 512],
              "woT": [1024, 1024], "mask4": [128, 4 * 512]}
    wd = {n: nc.dram_tensor(n, shapes[n], F32, kind="ExternalInput").ap()
          for n in names}
    ident_d = nc.dram_tensor("ident", [128, 128], F32, kind="ExternalInput").ap()
    xo_d = nc.dram_tensor("xo", [D, T], F32, kind="ExternalOutput").ap()
    with contextlib.ExitStack() as stack:
        p = Prog(nc, stack.enter_context)
        xt = [p.sbuf(f"xt{i}", [128, T], F32) for i in range(DC)]
        xln = [p.sbuf(f"xln{i}", [128, T], BF16) for i in range(DC)]
        ps = p.psum("ps", [128, 4096])
        scr_t = p.sbuf("scr", [128, 24 * 512], BF16)
        scr = scr_t[:]
        ones, ones_w, ident, ident_w, _ = emit_consts(p, ident_d)
        xw = emit_load_xt(p, xt, x_d)
        lnw = emit_ln(p, "ln", xt, xln, xw, ones=ones,
                      ones_waits=ones_w + ident_w, ps=ps, scr=scr)
        ln_mm = p.ctr["ln_mm"]
        ln_scr_free = [(ln_mm, ln_mm.v)]
        outw = emit_l0attn(p, "a0", xt, xln, lnw, wd, ones, ones_w, ps, scr,
                           ln_scr_free, ident, ident_w)
        emit_store_xt(p, xo_d, xt, outw)
        p.build_block()
    return nc


def emit_l1attn(p, tag, xt, wd, x_waits, ones, ones_waits, ps, scr, ident,
                ident_waits):
    NTB = T // 512
    NTC = T // 128
    NG, GH = 4, 4
    wdqT = p.sbuf(tag + "_wdqT", [128, DC * 512], BF16)
    wdkvT = p.sbuf(tag + "_wdkvT", [128, DC * 512], BF16)
    wuqT = p.sbuf(tag + "_wuqT", [128, 4 * 1024], BF16)
    wukT = p.sbuf(tag + "_wukT", [128, 4 * 1024], BF16)
    wuvT = p.sbuf(tag + "_wuvT", [128, 4 * 1024], BF16)
    woutT = p.sbuf(tag + "_woutT", [128, 2 * 1024], BF16)
    lnsc = p.sbuf(tag + "_lnsc", [128, DC], F32)
    lnbs = p.sbuf(tag + "_lnbs", [128, DC], F32)
    cqT = p.sbuf(tag + "_cqT", [128, 4 * T], BF16)
    ckvT = p.sbuf(tag + "_ckvT", [128, 4 * T], BF16)
    qg = p.sbuf(tag + "_qg", [128, 2 * T], BF16)
    kg = p.sbuf(tag + "_kg", [128, 2 * T], BF16)
    vg = p.sbuf(tag + "_vg", [128, NTC * 256], BF16)
    yg = p.sbuf(tag + "_yg", [128, 2 * T], BF16)
    ones32 = p.sbuf(tag + "_o32", [1, 128], F32)
    qtmp = None
    sq1 = scr[:, 0:DC * 512]
    xln_tb = scr[:, 0:DC * 512]
    bf1 = scr[:, DC * 512:2 * DC * 512]
    sq2 = scr[:, DC * 512:2 * DC * 512]
    h_tb = scr[:, DC * 512:2 * DC * 512]
    _fs = scr[:, 2 * DC * 512:2 * DC * 512 + 4096].bitcast(F32)
    mu, msq = _fs[:, 0:512], _fs[:, 512:1024]
    var = msq
    tmp = _fs[:, 1024:1536]
    rstd = tmp
    mur = _fs[:, 1536:2048]
    u = mu
    Pt = [scr[:, 0:512], scr[:, 512:1024]]
    cosb = scr[:, 6912:8960]
    sinb = scr[:, 8960:11008]
    qtmp = scr[:, 6144:6400]
    qrop = scr[:, 6400:6656]
    rta = scr[:, 6656:6784]
    rtb = scr[:, 6784:6912]
    _f2 = scr[:, 2048:6144].bitcast(F32)
    invsB = _f2[:, 0:512]
    t1 = _f2[:, 512:1024]
    lns = _f2[0:1, 1024:1536]
    invs = _f2[0:1, 1536:2048]

    ld = p.sem(tag + "_ld")
    mm = p.sem(tag + "_mm")
    ve = p.sem(tag + "_ve")
    sc = p.sem(tag + "_sc")
    gp = p.sem(tag + "_gp")

    ld_w = ld.inc(16 * 7)
    ldw2 = p.sem(tag + "_ldw2")
    ld_wo = [ldw2.inc(16) for _ in range(4)]
    ldc = p.sem(tag + "_ldc")
    ld_cs = ldc.inc(16 * 2)
    gp_o = gp.inc()
    sc_sq1 = [0] * NTB
    sc_r1 = [0] * NTB
    sc_sq2 = [0] * NTB
    sc_r2 = [0] * NTB
    for tb in range(NTB):
        sc_sq1[tb] = sc.inc()
        sc_r1[tb] = sc.inc()
        sc_sq2[tb] = sc.inc()
        sc_r2[tb] = sc.inc()
    mm_s1 = [0] * NTB
    mm_s2 = [0] * NTB
    mm_cq = [[0] * 4 for _ in range(NTB)]
    mm_ckv = [[0] * 4 for _ in range(NTB)]
    for tb in range(NTB):
        mm_s1[tb] = mm.inc()
        mm_s2[tb] = mm.inc()
        for qc in range(4):
            mm_cq[tb][qc] = mm.inc()
        for lc in range(4):
            mm_ckv[tb][lc] = mm.inc()
    ve_v1 = [0] * NTB
    ve_x1 = [0] * NTB
    ve_v2 = [0] * NTB
    ve_h = [0] * NTB
    ve_cq = [[0] * 4 for _ in range(NTB)]
    ve_ckv = [[0] * 4 for _ in range(NTB)]
    for tb in range(NTB):
        ve_v1[tb] = ve.inc()
        ve_x1[tb] = ve.inc()
        ve_v2[tb] = ve.inc()
        ve_h[tb] = ve.inc()
        for qc in range(4):
            ve_cq[tb][qc] = ve.inc()
        for lc in range(4):
            ve_ckv[tb][lc] = ve.inc()
    mm_qkv = [[[0] * 3 for _ in range(NTC)] for _ in range(NG)]
    mm_tq = [[[0] * 2 for _ in range(NTC)] for _ in range(NG)]
    mm_tk = [[[0] * 2 for _ in range(NTC)] for _ in range(NG)]
    mm_S = [[[[0] * NTC for _ in range(NTB)] for _ in range(GH)] for _ in range(NG)]
    mm_pv = [[[[0] * NTC for _ in range(NTB)] for _ in range(GH)] for _ in range(NG)]
    mm_ib = [[[0] * NTB for _ in range(GH)] for _ in range(NG)]
    mm_wo = [[[0] * NTB for _ in range(DC)] for _ in range(NG)]
    for g in range(NG):
        for tc in range(NTC):
            mm_qkv[g][tc][0] = mm.inc()
            for j in range(2):
                mm_tq[g][tc][j] = mm.inc()
            mm_qkv[g][tc][1] = mm.inc()
            for j in range(2):
                mm_tk[g][tc][j] = mm.inc()
            mm_qkv[g][tc][2] = mm.inc()
        for lh in range(GH):
            for qb in range(NTB):
                mm_S[g][lh][qb][0] = mm.inc()
                for kc in range(NTC):
                    if kc + 1 < NTC:
                        mm_S[g][lh][qb][kc + 1] = mm.inc()
                    mm_pv[g][lh][qb][kc] = mm.inc()
                mm_ib[g][lh][qb] = mm.inc()
        for md in range(DC):
            for tb in range(NTB):
                mm_wo[g][md][tb] = mm.inc()
    sc_p = [[[[0] * NTC for _ in range(NTB)] for _ in range(GH)] for _ in range(NG)]
    sc_inv = [[[0] * NTB for _ in range(GH)] for _ in range(NG)]
    for g in range(NG):
        for lh in range(GH):
            for qb in range(NTB):
                for kc in range(NTC):
                    sc_p[g][lh][qb][kc] = sc.inc()
                sc_inv[g][lh][qb] = sc.inc()
    ve_q1 = [[0] * NTC for _ in range(NG)]
    ve_q2 = [[0] * NTC for _ in range(NG)]
    ve_q3 = [[0] * NTC for _ in range(NG)]
    ve_k1 = [[0] * NTC for _ in range(NG)]
    ve_k2 = [[0] * NTC for _ in range(NG)]
    ve_k3 = [[0] * NTC for _ in range(NG)]
    ve_v3 = [[0] * NTC for _ in range(NG)]
    ve_yg = [[[0] * NTB for _ in range(GH)] for _ in range(NG)]
    ve_wo = [[[0] * NTB for _ in range(DC)] for _ in range(NG)]
    for g in range(NG):
        for tc in range(NTC):
            ve_q1[g][tc] = ve.inc()
            ve_q2[g][tc] = ve.inc()
            ve_q3[g][tc] = ve.inc()
            ve_k1[g][tc] = ve.inc()
            ve_k2[g][tc] = ve.inc()
            ve_k3[g][tc] = ve.inc()
            ve_v3[g][tc] = ve.inc()
        for lh in range(GH):
            for qb in range(NTB):
                ve_yg[g][lh][qb] = ve.inc()
        for md in range(DC):
            for tb in range(NTB):
                ve_wo[g][md][tb] = ve.inc()

    def g_fn(g_):
        g_.memset(ones32[:], 1.0).then_inc(gp.sem, 1)
        for nm, dst in [("wdqT", wdqT), ("wdkvT", wdkvT), ("wuqT", wuqT),
                        ("wukT", wukT), ("wuvT", wuvT)]:
            g_.dma_start(dst[:], wd[nm].rearrange("(k p) f -> p k f", p=128)
                         ).then_inc(ld.sem, 16)
        g_.dma_start(lnsc[:], wd["lnsc"]).then_inc(ld.sem, 16)
        g_.dma_start(lnbs[:], wd["lnbs"]).then_inc(ld.sem, 16)
        g_.dma_start(
            woutT[:],
            wd["woutT"][0:256, :].rearrange(
                "(k p) f -> p k f", p=128)).then_inc(ldw2.sem, 16)
        g_.wait_ge(mm.sem, mm_ckv[NTB - 1][3])
        g_.dma_start(cosb, wd["cosR"].rearrange("(k p) f -> p k f", p=128)
                     ).then_inc(ldc.sem, 16)
        g_.dma_start(sinb, wd["sinR"].rearrange("(k p) f -> p k f", p=128)
                     ).then_inc(ldc.sem, 16)
        for g2 in range(1, 4):
            g_.wait_ge(mm.sem, mm_wo[g2 - 1][DC - 1][NTB - 1])
            g_.dma_start(
                woutT[:],
                wd["woutT"][g2 * 256:(g2 + 1) * 256, :].rearrange(
                    "(k p) f -> p k f", p=128)).then_inc(ldw2.sem, 16)
    p.add("gpsimd", g_fn)

    def t_fn(t):
        wait_all(t, ones_waits)
        wait_all(t, ident_waits)
        wait_all(t, x_waits)
        t.wait_ge(ld.sem, ld_w)
        ps_bf = ps[:].bitcast(BF16)
        for tb in range(NTB):
            t.wait_ge(sc.sem, sc_sq1[tb])
            if tb > 0:
                t.wait_ge(ve.sem, ve_v1[tb - 1])
            for i in range(DC):
                csl = slice(i * 512, (i + 1) * 512)
                t.matmul(ps[:, 0:512], ones[:], bf1[:, csl],
                         start=(i == 0), stop=(i == DC - 1))
            for i in range(DC):
                csl = slice(i * 512, (i + 1) * 512)
                inst = t.matmul(ps[:, 512:1024], ones[:], sq1[:, csl],
                                start=(i == 0), stop=(i == DC - 1))
            inst.then_inc(mm.sem, 1)
            t.wait_ge(sc.sem, sc_sq2[tb])
            t.wait_ge(ve.sem, ve_x1[tb])
            for i in range(DC):
                csl = slice(i * 512, (i + 1) * 512)
                t.matmul(ps[:, 0:512], ones[:], xln_tb[:, csl],
                         start=(i == 0), stop=(i == DC - 1))
            for i in range(DC):
                csl = slice(i * 512, (i + 1) * 512)
                inst = t.matmul(ps[:, 512:1024], ones[:], sq2[:, csl],
                                start=(i == 0), stop=(i == DC - 1))
            inst.then_inc(mm.sem, 1)
            t.wait_ge(ve.sem, ve_h[tb])
            gi = 0
            for qc in range(4):
                bank = 1024 + (gi % 2) * 512
                if gi >= 2:
                    t.wait_ge(ve.sem, ve_cq[tb][gi - 2])
                elif tb > 0:
                    t.wait_ge(ve.sem, ve_ckv[tb - 1][2 + gi])
                for c in range(DC):
                    inst = t.matmul(
                        ps[:, bank:bank + 512],
                        wdqT[:, c * 512 + qc * 128: c * 512 + (qc + 1) * 128],
                        h_tb[:, c * 512:(c + 1) * 512],
                        start=(c == 0), stop=(c == DC - 1))
                inst.then_inc(mm.sem, 1)
                gi += 1
            gi = 0
            for lc in range(4):
                bank = 1024 + (gi % 2) * 512
                if gi >= 2:
                    t.wait_ge(ve.sem, ve_ckv[tb][gi - 2])
                else:
                    t.wait_ge(ve.sem, ve_cq[tb][2 + gi])
                for c in range(DC):
                    inst = t.matmul(
                        ps[:, bank:bank + 512],
                        wdkvT[:, c * 512 + lc * 128: c * 512 + (lc + 1) * 128],
                        h_tb[:, c * 512:(c + 1) * 512],
                        start=(c == 0), stop=(c == DC - 1))
                inst.then_inc(mm.sem, 1)
                gi += 1
        for g in range(NG):
            gsl = slice(g * 256, (g + 1) * 256)
            for tc in range(NTC):
                if not (g == 0 and tc == 0):
                    pg, ptc = (g, tc - 1) if tc > 0 else (g - 1, NTC - 1)
                    t.wait_ge(ve.sem, ve_v3[pg][ptc])
                else:
                    t.wait_ge(ve.sem, ve_ckv[NTB - 1][3])
                for kq in range(4):
                    inst = t.matmul(
                        ps[:, 0:256],
                        cqT[:, kq * T + tc * 128: kq * T + (tc + 1) * 128],
                        wuqT[:, kq * 1024:(kq + 1) * 1024][:, gsl],
                        start=(kq == 0), stop=(kq == 3))
                inst.then_inc(mm.sem, 1)
                t.wait_ge(ve.sem, ve_q2[g][tc])
                if not (g == 0 and tc == 0):
                    pg, ptc = (g, tc - 1) if tc > 0 else (g - 1, NTC - 1)
                    t.wait_ge(ve.sem, ve_k3[pg][ptc])
                for j in range(2):
                    t.transpose(ps_bf[:, 1024 + j * 128:1024 + (j + 1) * 128],
                                qrop[:, j * 128:(j + 1) * 128],
                                ident[:]).then_inc(mm.sem, 1)
                t.wait_ge(ve.sem, ve_q1[g][tc])
                for kq in range(4):
                    inst = t.matmul(
                        ps[:, 0:256],
                        ckvT[:, kq * T + tc * 128: kq * T + (tc + 1) * 128],
                        wukT[:, kq * 1024:(kq + 1) * 1024][:, gsl],
                        start=(kq == 0), stop=(kq == 3))
                inst.then_inc(mm.sem, 1)
                t.wait_ge(ve.sem, ve_k2[g][tc])
                t.wait_ge(ve.sem, ve_q3[g][tc])
                for j in range(2):
                    t.transpose(ps_bf[:, 1024 + j * 128:1024 + (j + 1) * 128],
                                qrop[:, j * 128:(j + 1) * 128],
                                ident[:]).then_inc(mm.sem, 1)
                t.wait_ge(ve.sem, ve_k1[g][tc])
                for kq in range(4):
                    inst = t.matmul(
                        ps[:, 0:256],
                        ckvT[:, kq * T + tc * 128: kq * T + (tc + 1) * 128],
                        wuvT[:, kq * 1024:(kq + 1) * 1024][:, gsl],
                        start=(kq == 0), stop=(kq == 3))
                inst.then_inc(mm.sem, 1)
            for lh in range(GH):
                base = (lh % 2) * 64
                cof = (lh // 2) * T
                for qb in range(NTB):
                    qsl = slice(qb * 512, (qb + 1) * 512)

                    def s_mm(t, kc):
                        bank = 1024 + (kc % 2) * 512
                        t.matmul(ps[:, bank:bank + 512],
                                 kg[base:base + 64, cof + kc * 128: cof + (kc + 1) * 128],
                                 qg[base:base + 64, cof + qb * 512: cof + (qb + 1) * 512],
                                 start=True, stop=True).then_inc(mm.sem, 1)
                    if lh == 0 and qb == 0:
                        t.wait_ge(ve.sem, ve_v3[g][NTC - 1])
                        t.wait_ge(ve.sem, ve_q3[g][NTC - 1])
                        t.wait_ge(ve.sem, ve_k3[g][NTC - 1])
                    if not (g == 0 and lh == 0 and qb == 0):
                        pg, plh, pqb = (g, lh, qb - 1) if qb > 0 else \
                            ((g, lh - 1, NTB - 1) if lh > 0 else (g - 1, GH - 1, NTB - 1))
                        t.wait_ge(sc.sem, sc_p[pg][plh][pqb][NTC - 1])
                        t.wait_ge(ve.sem, ve_yg[pg][plh][pqb])
                    s_mm(t, 0)
                    for kc in range(NTC):
                        if kc + 1 < NTC:
                            if kc >= 1:
                                t.wait_ge(sc.sem, sc_p[g][lh][qb][kc - 1])
                            s_mm(t, kc + 1)
                        t.wait_ge(sc.sem, sc_p[g][lh][qb][kc])
                        t.matmul(ps[base:base + 64, 2048:2560],
                                 vg[:, kc * 256 + lh * 64: kc * 256 + (lh + 1) * 64],
                                 Pt[kc % 2],
                                 start=(kc == 0), stop=(kc == NTC - 1))
                        t.matmul(ps[0:1, 2560:3072], ones[:, 0:1], Pt[kc % 2],
                                 start=(kc == 0),
                                 stop=(kc == NTC - 1)).then_inc(mm.sem, 1)
                    t.wait_ge(sc.sem, sc_inv[g][lh][qb])
                    t.matmul(ps[:, 2560:3072], ones32[:], invs,
                             start=True, stop=True).then_inc(mm.sem, 1)
            t.wait_ge(ldw2.sem, ld_wo[g])
            for md in range(DC):
                for tb in range(NTB):
                    gi = md * NTB + tb
                    bank = 3072 + (gi % 2) * 512
                    if gi >= 2:
                        t.wait_ge(ve.sem, ve_wo[g][(gi - 2) // NTB][(gi - 2) % NTB])
                    elif g > 0:
                        t.wait_ge(ve.sem, ve_wo[g - 1][DC - 1][NTB - 2 + gi])
                    if gi == 0:
                        t.wait_ge(ve.sem, ve_yg[g][GH - 1][NTB - 1])
                    for j in range(2):
                        inst = t.matmul(
                            ps[:, bank:bank + 512],
                            woutT[:, j * 1024 + md * 128: j * 1024 + (md + 1) * 128],
                            yg[:, j * T + tb * 512: j * T + (tb + 1) * 512],
                            start=(j == 0), stop=(j == 1))
                    inst.then_inc(mm.sem, 1)
    p.add("tensor", t_fn)

    def s_fn(s):
        for tb in range(NTB):
            sl = slice(tb * 512, (tb + 1) * 512)
            wait_all(s, x_waits) if tb == 0 else None
            if tb > 0:
                s.wait_ge(ve.sem, ve_h[tb - 1])
                s.wait_ge(mm.sem, mm_ckv[tb - 1][3])
            for i in range(DC):
                csl = slice(i * 512, (i + 1) * 512)
                s.activation(sq1[:, csl], xt[i][:, sl], AF.Square)
                s.activation(bf1[:, csl], xt[i][:, sl], AF.Copy)
            s.drain().then_inc(sc.sem, 1)
            s.wait_ge(ve.sem, ve_v1[tb])
            s.activation(rstd, var, AF.Ln, bias=p.cb[:, 0:1])
            s.drain()
            s.activation(rstd, rstd, AF.Exp, scale=-0.5)
            s.drain().then_inc(sc.sem, 1)
            s.wait_ge(ve.sem, ve_x1[tb])
            s.wait_ge(mm.sem, mm_s2[tb - 1] if tb > 0 else 0)
            for i in range(DC):
                csl = slice(i * 512, (i + 1) * 512)
                s.activation(sq2[:, csl], xln_tb[:, csl], AF.Square)
            s.drain().then_inc(sc.sem, 1)
            s.wait_ge(ve.sem, ve_v2[tb])
            s.activation(rstd, var, AF.Ln, bias=p.cb[:, 0:1])
            s.drain()
            s.activation(rstd, rstd, AF.Exp, scale=-0.5)
            s.drain().then_inc(sc.sem, 1)
        for g in range(NG):
            for lh in range(GH):
                for qb in range(NTB):
                    for kc in range(NTC):
                        s.wait_ge(mm.sem, mm_S[g][lh][qb][kc])
                        if kc >= 2:
                            s.wait_ge(mm.sem, mm_pv[g][lh][qb][kc - 2])
                        elif not (g == 0 and lh == 0 and qb == 0):
                            pg, plh, pqb = (g, lh, qb - 1) if qb > 0 else \
                                ((g, lh - 1, NTB - 1) if lh > 0 else
                                 (g - 1, GH - 1, NTB - 1))
                            s.wait_ge(mm.sem, mm_pv[pg][plh][pqb][NTC - 1])
                        bank = 1024 + (kc % 2) * 512
                        s.activation(Pt[kc % 2], ps[:, bank:bank + 512],
                                     AF.Exp, scale=0.125, bias=p.cb[:, 1:2])
                        s.drain().then_inc(sc.sem, 1)
                    s.wait_ge(mm.sem, mm_pv[g][lh][qb][NTC - 1])
                    s.activation(lns, ps[0:1, 2560:3072], AF.Ln)
                    s.drain()
                    s.activation(invs, lns, AF.Exp, scale=-1.0)
                    s.drain().then_inc(sc.sem, 1)
    p.add("scalar", s_fn)

    def v_fn(v):
        ps_bf = ps[:].bitcast(BF16)
        for tb in range(NTB):
            sl = slice(tb * 512, (tb + 1) * 512)
            v.wait_ge(mm.sem, mm_s1[tb])
            v.tensor_scalar(mu, ps[:, 0:512], 1.0 / D, None, op0=ALU.mult)
            v.tensor_scalar(msq, ps[:, 512:1024], 1.0 / D, None, op0=ALU.mult)
            v.drain()
            v.tensor_tensor(tmp, mu, mu, op=ALU.mult)
            v.drain()
            v.tensor_tensor(var, msq, tmp, op=ALU.subtract)
            v.drain().then_inc(ve.sem, 1)
            v.wait_ge(sc.sem, sc_r1[tb])
            v.tensor_tensor(mur, mu, rstd, op=ALU.mult)
            v.drain()
            for i in range(DC):
                csl = slice(i * 512, (i + 1) * 512)
                v.tensor_tensor(u, xt[i][:, sl], rstd, op=ALU.mult)
                v.drain()
                v.tensor_tensor(xln_tb[:, csl], u, mur, op=ALU.subtract)
                v.drain()
                v.tensor_tensor(xt[i][:, sl], xt[i][:, sl], xln_tb[:, csl],
                                op=ALU.add)
                v.drain()
            v.nop().then_inc(ve.sem, 1)
            v.wait_ge(mm.sem, mm_s2[tb])
            v.tensor_scalar(mu, ps[:, 0:512], 1.0 / D, None, op0=ALU.mult)
            v.tensor_scalar(msq, ps[:, 512:1024], 1.0 / D, None, op0=ALU.mult)
            v.drain()
            v.tensor_tensor(tmp, mu, mu, op=ALU.mult)
            v.drain()
            v.tensor_tensor(var, msq, tmp, op=ALU.subtract)
            v.drain().then_inc(ve.sem, 1)
            v.wait_ge(sc.sem, sc_r2[tb])
            v.tensor_tensor(mur, mu, rstd, op=ALU.mult)
            v.drain()
            for i in range(DC):
                csl = slice(i * 512, (i + 1) * 512)
                v.tensor_tensor(u, xln_tb[:, csl], rstd, op=ALU.mult)
                v.drain()
                v.tensor_tensor(u, u, mur, op=ALU.subtract)
                v.drain()
                v.tensor_scalar(h_tb[:, csl], u, lnsc[:, i:i + 1],
                                lnbs[:, i:i + 1], op0=ALU.mult, op1=ALU.add)
                v.drain()
            v.nop().then_inc(ve.sem, 1)
            gi = 0
            for qc in range(4):
                bank = 1024 + (gi % 2) * 512
                v.wait_ge(mm.sem, mm_cq[tb][qc])
                v.tensor_copy(cqT[:, qc * T + tb * 512: qc * T + (tb + 1) * 512],
                              ps[:, bank:bank + 512])
                v.drain().then_inc(ve.sem, 1)
                gi += 1
            gi = 0
            for lc in range(4):
                bank = 1024 + (gi % 2) * 512
                v.wait_ge(mm.sem, mm_ckv[tb][lc])
                v.tensor_copy(ckvT[:, lc * T + tb * 512: lc * T + (tb + 1) * 512],
                              ps[:, bank:bank + 512])
                v.drain().then_inc(ve.sem, 1)
                gi += 1
        v.wait_ge(ldc.sem, ld_cs)
        for g in range(NG):
            for tc in range(NTC):
                cs = slice(tc * 128, (tc + 1) * 128)
                v.wait_ge(mm.sem, mm_qkv[g][tc][0])
                v.tensor_copy(qtmp, ps[:, 0:256])
                v.drain().then_inc(ve.sem, 1)
                v.tensor_tensor(rta, qtmp[0:128, 0:256:2], cosb[:, cs], op=ALU.mult)
                v.tensor_tensor(rtb, qtmp[0:128, 1:256:2], sinb[:, cs], op=ALU.mult)
                v.drain()
                v.tensor_tensor(qrop[:, 0:256:2], rta, rtb, op=ALU.subtract)
                v.drain()
                v.tensor_tensor(rta, qtmp[0:128, 0:256:2], sinb[:, cs], op=ALU.mult)
                v.tensor_tensor(rtb, qtmp[0:128, 1:256:2], cosb[:, cs], op=ALU.mult)
                v.drain()
                v.tensor_tensor(qrop[:, 1:256:2], rta, rtb, op=ALU.add)
                v.drain().then_inc(ve.sem, 1)
                v.wait_ge(mm.sem, mm_tq[g][tc][1])
                for j in range(2):
                    v.tensor_copy(qg[:, j * T + tc * 128: j * T + (tc + 1) * 128],
                                  ps_bf[:, 1024 + j * 128:1024 + (j + 1) * 128])
                v.drain().then_inc(ve.sem, 1)
                v.wait_ge(mm.sem, mm_qkv[g][tc][1])
                v.tensor_copy(qtmp, ps[:, 0:256])
                v.drain().then_inc(ve.sem, 1)
                v.tensor_tensor(rta, qtmp[0:128, 0:256:2], cosb[:, cs], op=ALU.mult)
                v.tensor_tensor(rtb, qtmp[0:128, 1:256:2], sinb[:, cs], op=ALU.mult)
                v.drain()
                v.tensor_tensor(qrop[:, 0:256:2], rta, rtb, op=ALU.subtract)
                v.drain()
                v.tensor_tensor(rta, qtmp[0:128, 0:256:2], sinb[:, cs], op=ALU.mult)
                v.tensor_tensor(rtb, qtmp[0:128, 1:256:2], cosb[:, cs], op=ALU.mult)
                v.drain()
                v.tensor_tensor(qrop[:, 1:256:2], rta, rtb, op=ALU.add)
                v.drain().then_inc(ve.sem, 1)
                v.wait_ge(mm.sem, mm_tk[g][tc][1])
                for j in range(2):
                    v.tensor_copy(kg[:, j * T + tc * 128: j * T + (tc + 1) * 128],
                                  ps_bf[:, 1024 + j * 128:1024 + (j + 1) * 128])
                v.drain().then_inc(ve.sem, 1)
                v.wait_ge(mm.sem, mm_qkv[g][tc][2])
                v.tensor_copy(vg[:, tc * 256:(tc + 1) * 256], ps[:, 0:256])
                v.drain().then_inc(ve.sem, 1)
            for lh in range(GH):
                base = (lh % 2) * 64
                cof = (lh // 2) * T
                for qb in range(NTB):
                    qsl = slice(qb * 512, (qb + 1) * 512)
                    v.wait_ge(mm.sem, mm_ib[g][lh][qb])
                    v.tensor_copy(invsB, ps[:, 2560:3072])
                    v.drain()
                    v.tensor_tensor(yg[base:base + 64, cof + qb * 512: cof + (qb + 1) * 512],
                                    ps[base:base + 64, 2048:2560],
                                    invsB[base:base + 64, :], op=ALU.mult)
                    v.drain().then_inc(ve.sem, 1)
            for md in range(DC):
                for tb in range(NTB):
                    sl = slice(tb * 512, (tb + 1) * 512)
                    gi = md * NTB + tb
                    bank = 3072 + (gi % 2) * 512
                    v.wait_ge(mm.sem, mm_wo[g][md][tb])
                    v.tensor_tensor(xt[md][:, sl], xt[md][:, sl],
                                    ps[:, bank:bank + 512], op=ALU.add)
                    v.drain().then_inc(ve.sem, 1)
    p.add("vector", v_fn)

    return [(p.ctr[tag + "_ve"], ve_wo[NG - 1][DC - 1][NTB - 1])]


def build_l1attn_launch():
    nc = bass.Bass("TRN2", target_bir_lowering=False)
    x_d = nc.dram_tensor("x", [D, T], F32, kind="ExternalInput").ap()
    shapes = {"wdqT": [1024, 512], "wdkvT": [1024, 512], "wuqT": [512, 1024],
              "wukT": [512, 1024], "wuvT": [512, 1024], "woutT": [1024, 1024],
              "lnsc": [128, DC], "lnbs": [128, DC],
              "cosR": [2048, 128], "sinR": [2048, 128]}
    wd = {n: nc.dram_tensor(n, s, F32, kind="ExternalInput").ap()
          for n, s in shapes.items()}
    ident_d = nc.dram_tensor("ident", [128, 128], F32, kind="ExternalInput").ap()
    xo_d = nc.dram_tensor("xo", [D, T], F32, kind="ExternalOutput").ap()
    with contextlib.ExitStack() as stack:
        p = Prog(nc, stack.enter_context)
        xt = [p.sbuf(f"xt{i}", [128, T], F32) for i in range(DC)]
        ps = p.psum("ps", [128, 4096])
        scr_t = p.sbuf("scr", [128, 24 * 512], BF16)
        scr = scr_t[:]
        ones, ones_w, ident, ident_w, _ = emit_consts(p, ident_d)
        xw = emit_load_xt(p, xt, x_d)
        outw = emit_l1attn(p, "b1", xt, wd, xw, ones, ones_w, ps, scr,
                           ident, ident_w)
        emit_store_xt(p, xo_d, xt, outw)
        p.build_block()
    return nc


def emit_l1ffn(p, tag, xt, dd, wd, x_waits, ones, ones_waits, ps, scr, ident,
               ident_waits, core_id_unused=None):
    NTB = T // 512
    NTC = T // 128
    EL = 4
    C = C1
    BIGF = 1.0e6
    xln_tm = p.sbuf(tag + "_xtm", [128, NTC * 1024], BF16)
    wrt = p.sbuf(tag + "_wrt", [128, DC * 32], BF16)
    wslot = p.sbuf(tag + "_wsl", [128, DC * 1024], BF16)
    bufT = p.sbuf(tag + "_bufT", [128, DC * C], BF16)
    gtile = p.sbuf(tag + "_gt", [128, 1024], BF16)
    gsb = p.sbuf(tag + "_gsb", [128, DC * C], BF16)
    usb = p.sbuf(tag + "_usb", [128, DC * C], BF16)
    ytm = p.sbuf(tag + "_ytm", [128, 1024], BF16)
    ohT = p.sbuf(tag + "_ohT", [E1, T], BF16)
    sones = p.sbuf(tag + "_sones", [E1, T], BF16)
    cums = p.sbuf(tag + "_cums", [E1, T], F32)
    cumtm = p.sbuf(tag + "_cumtm", [128, 32], F32)
    oh_sv = p.sbuf(tag + "_ohsv", [128, NTC * 32], BF16)
    eidx = p.sbuf(tag + "_eidx", [128, NTC], F32)
    gate = p.sbuf(tag + "_gate", [128, NTC], F32)
    gatef = p.sbuf(tag + "_gatef", [128, NTC], F32)
    sidx = p.sbuf(tag + "_sidx", [128, NTC], U32)
    sidx2 = p.sbuf(tag + "_sidx2", [128, NTC], U32)
    iotap = p.sbuf(tag + "_iotap", [128, 1], I32)
    r_t5 = p.sbuf(tag + "_rt5", [128, 1], F32)
    tokix = p.sbuf(tag + "_tokix", [128, 20], U32)
    toksb = p.sbuf(tag + "_toksb", [128, 20], I32)
    tokf = p.sbuf(tag + "_tokf", [128, 20], F32)
    tokg = p.sbuf(tag + "_tokg", [128, 20], F32)
    ids_sb = p.sbuf(tag + "_ids", [128, NTC], I32)
    moff = p.sbuf(tag + "_moff", [1, 1], I32)
    zz = p.sbuf(tag + "_zz", [128, 161], I32)
    iotaE = p.sbuf(tag + "_iotaE", [128, 32], I32)
    rmask = p.sbuf(tag + "_rmask", [E1, 8], F32)
    tot_sb = p.sbuf(tag + "_totsb", [E1, 8], F32)
    offs = p.sbuf(tag + "_offs", [E1, 1], F32)
    offsR = p.sbuf(tag + "_offsR", [1, 32], F32)
    offsB = p.sbuf(tag + "_offsB", [128, 32], F32)
    ident32 = p.sbuf(tag + "_id32", [128, 128], F32)
    ones32 = p.sbuf(tag + "_o32f", [1, 128], F32)
    r_m = p.sbuf(tag + "_rm", [128, 1], F32)
    r_nm = p.sbuf(tag + "_rnm", [128, 1], F32)
    r_e = p.sbuf(tag + "_re", [128, 32], F32)
    r_s = p.sbuf(tag + "_rs", [128, 1], F32)
    r_oh = p.sbuf(tag + "_roh", [128, 32], F32)
    r_t1 = p.sbuf(tag + "_rt1", [128, 1], F32)
    r_t2 = p.sbuf(tag + "_rt2", [128, 1], F32)
    r_t3 = p.sbuf(tag + "_rt3", [128, 1], F32)
    r_t4 = p.sbuf(tag + "_rt4", [128, 1], F32)
    sq1 = scr[:, 0:DC * 512]
    xln_tb = scr[:, 0:DC * 512]
    bf1 = scr[:, DC * 512:2 * DC * 512]
    _fs = scr[:, 2 * DC * 512:2 * DC * 512 + 4096].bitcast(F32)
    mu, msq = _fs[:, 0:512], _fs[:, 512:1024]
    var = msq
    tmp = _fs[:, 1024:1536]
    rstd = tmp
    mur = _fs[:, 1536:2048]
    u = mu
    ztile = scr[:, DC * 512:DC * 512 + 1024]

    LVL = int(globals().get("L1FFN_LEVEL", 50))
    ld = p.sem(tag + "_ld")
    ldw = p.sem(tag + "_ldw")
    lde = p.sem(tag + "_lde")
    mm = p.sem(tag + "_mm")
    ve = p.sem(tag + "_ve")
    sc = p.sem(tag + "_sc")
    gp = p.sem(tag + "_gp")
    cc = p.sem(tag + "_cc")
    idm = p.sem(tag + "_idm")

    ld_c = ld.inc(16 * 5)
    gp_c = gp.inc()
    sc_sq1 = [0] * NTB
    sc_r1 = [0] * NTB
    sc_exp = [[0] * 4 for _ in range(NTB)]
    for tb in range(NTB):
        sc_sq1[tb] = sc.inc()
        sc_r1[tb] = sc.inc()
        for j in range(4):
            sc_exp[tb][j] = sc.inc()
    mm_s1 = [0] * NTB
    mm_tp = [[0] * 4 for _ in range(NTB)]
    mm_lg = [[0] * 4 for _ in range(NTB)]
    mm_ohT = [[0] * 4 for _ in range(NTB)]
    for tb in range(NTB):
        mm_s1[tb] = mm.inc()
        for j in range(4):
            mm_tp[tb][j] = mm.inc()
            mm_lg[tb][j] = mm.inc()
            mm_ohT[tb][j] = mm.inc()
    mm_toff = mm.inc()
    mm_boff = mm.inc()
    mm_ctm = [0] * NTC
    for tc in range(NTC):
        mm_ctm[tc] = mm.inc()
    mm_tb2 = [[0] * 5 for _ in range(EL)]
    mm_g = [[0] * DC for _ in range(EL)]
    mm_u = [[0] * DC for _ in range(EL)]
    mm_yy = [[0] * DC for _ in range(EL)]
    mm_ty = [[0] * 5 for _ in range(EL)]
    for e in range(EL):
        for a in range(5):
            mm_tb2[e][a] = mm.inc()
        for mf in range(DC):
            mm_g[e][mf] = mm.inc()
        for mf in range(DC):
            mm_u[e][mf] = mm.inc()
        for mf in range(DC):
            mm_yy[e][mf] = mm.inc()
        for a in range(5):
            mm_ty[e][a] = mm.inc()
    mm_tz = [[0] * DC for _ in range(NTC)]
    for tc in range(NTC):
        for c2 in range(DC):
            mm_tz[tc][c2] = mm.inc()
    sc_sil = [[0] * DC for _ in range(EL)]
    for e in range(EL):
        for mf in range(DC):
            sc_sil[e][mf] = sc.inc()
    ve_v1 = [0] * NTB
    ve_x1 = [0] * NTB
    ve_tm = [[0] * 4 for _ in range(NTB)]
    ve_rt = [[0] * 4 for _ in range(NTB)]
    ve_ohT = [[0] * 4 for _ in range(NTB)]
    for tb in range(NTB):
        ve_v1[tb] = ve.inc()
        ve_x1[tb] = ve.inc()
        for j in range(4):
            ve_tm[tb][j] = ve.inc()
            ve_rt[tb][j] = ve.inc()
            ve_ohT[tb][j] = ve.inc()
    ve_scan = ve.inc()
    ve_off1 = ve.inc()
    ve_offR = ve.inc()
    ve_offB = ve.inc()
    ve_slot = [0] * NTC
    for tc in range(NTC):
        ve_slot[tc] = ve.inc()
    ve_tok = ve.inc()
    ve_bt = [[0] * 5 for _ in range(EL)]
    ve_gu = [[0] * DC for _ in range(EL)]
    ve_ysb = [[0] * DC for _ in range(EL)]
    ve_ytm = [[0] * 5 for _ in range(EL)]
    for e in range(EL):
        for a in range(5):
            ve_bt[e][a] = ve.inc()
        for mf in range(DC):
            ve_gu[e][mf] = ve.inc()
        for mf in range(DC):
            ve_ysb[e][mf] = ve.inc()
        for a in range(5):
            ve_ytm[e][a] = ve.inc()
    ve_z = [0] * NTC
    ve_fin = [[0] * DC for _ in range(NTC)]
    for tc in range(NTC):
        ve_z[tc] = ve.inc()
        for c2 in range(DC):
            ve_fin[tc][c2] = ve.inc()
    ld_we = [[0] * 3 for _ in range(EL)]
    ld_we[0][0] = ldw.inc(16)
    for e in range(EL):
        ld_we[e][1] = ldw.inc(16)
        ld_we[e][2] = ldw.inc(16)
        if e + 1 < EL:
            ld_we[e + 1][0] = ldw.inc(16)
    gp_c2 = None
    gp_xz = gp.inc()
    gp_ms = {}
    _g = [gp_xz]

    def gnext():
        _g[0] = gp.inc()
        return _g[0]
    gp_gth = [[0] * 5 for _ in range(EL)]
    for e in range(EL):
        for a in range(5):
            gp_gth[e][a] = gnext()
    gp_yg = [0] * NTC
    for tc in range(NTC):
        gp_yg[tc] = gnext()
    cc_xg = 1
    cc_tot = 2
    cc_tbl = 3
    cc_y = 4
    lde_n = [16 * (i + 1) for i in range(10)]

    def g_fn(g_):
        g_.dma_start(wrt[:], wd["wrt"].rearrange("(k p) f -> p k f", p=128)
                     ).then_inc(ld.sem, 16)
        g_.dma_start(rmask[:], wd["rmask"]).then_inc(ld.sem, 16)
        g_.dma_start(ids_sb[:], wd["ids"]).then_inc(ld.sem, 16)
        g_.dma_start(ident32[:], wd["ident32"]).then_inc(ld.sem, 16)
        g_.dma_start(moff[:], wd["myoff"]).then_inc(ld.sem, 16)
        g_.memset(ones32[:], 1.0)
        g_.memset(sones[:], 1.0)
        g_.memset(zz[:], 0)
        g_.iota(iotaE[:], pattern=[[1, 32]], base=0, channel_multiplier=0)
        g_.iota(iotap[:], pattern=[[0, 1]], base=0, channel_multiplier=1)
        g_.drain().then_inc(gp.sem, 1)
        g_.dma_start(wslot[:], wd["wg3"][0].rearrange("(k p) f -> p k f", p=128),
                     max_dma_last_dim=8192).then_inc(ldw.sem, 16)
        if LVL < 3:
            return
        g_.wait_ge(ve.sem, ve_tm[NTB - 1][3])
        g_.dma_start(dd["xg_my"].ap().rearrange("(a p) f -> p a f", p=128),
                     xln_tm[:]).then_inc(lde.sem, 16)
        g_.wait_ge(lde.sem, lde_n[0])
        g_.collective_compute("AllGather", ALU.bypass,
                              replica_groups=[list(range(8))],
                              ins=[dd["xg_my"].ap()],
                              outs=[dd["xg_all"].ap()]).then_inc(cc.sem)
        g_.wait_ge(ve.sem, ve_scan)
        g_.dma_start(dd["tot_my"].ap(), cums[:, T - 1:T]).then_inc(lde.sem, 16)
        g_.wait_ge(lde.sem, lde_n[1])
        g_.collective_compute("AllGather", ALU.bypass,
                              replica_groups=[list(range(8))],
                              ins=[dd["tot_my"].ap()],
                              outs=[dd["tot_all"].ap()]).then_inc(cc.sem)
        g_.wait_ge(cc.sem, cc_tot)
        g_.dma_start(tot_sb[:], dd["tot_all"].ap().rearrange(
            "(r e) one -> e (r one)", e=E1),
            allow_slow_non_contiguous=True).then_inc(lde.sem, 16)
        g_.dma_start(dd["tbl_my"].ap().rearrange("(p a) one -> p (a one)", p=128),
                     zz[:]).then_inc(lde.sem, 16)
        g_.wait_ge(lde.sem, lde_n[3])
        for tc in range(NTC):
            g_.wait_ge(ve.sem, ve_slot[tc])
            g_.indirect_dma_start(
                dd["tbl_my"].ap(),
                bass.IndirectOffsetOnAxis(ap=sidx[:, tc:tc + 1], axis=0),
                ids_sb[:, tc:tc + 1], None).then_inc(idm.sem, 16)
        g_.wait_ge(idm.sem, 16 * NTC)
        g_.collective_compute("AllReduce", ALU.add,
                              replica_groups=[list(range(8))],
                              ins=[dd["tbl_my"].ap()],
                              outs=[dd["tblr"].ap()]).then_inc(cc.sem)
        g_.wait_ge(cc.sem, cc_tbl)
        g_.wait_ge(ld.sem, ld_c)
        with g_.register(tag + "_gro") as gro:
            g_.reg_load(gro, moff[0:1, 0:1])
            g_.dma_start(toksb[:],
                         bass.AP(dd["tblr"], gro, [[1, 128], [128, 20], [1, 1]]),
                         allow_slow_non_contiguous=True).then_inc(lde.sem, 16)
        g_.wait_ge(lde.sem, lde_n[4])
        g_.drain().then_inc(gp.sem, 1)
        if LVL < 4:
            return
        for e in range(EL):
            for a in range(5):
                g_.wait_ge(ve.sem, ve_tok)
                if not (e == 0 and a == 0):
                    pe, pa = (e, a - 1) if a > 0 else (e - 1, 4)
                    g_.wait_ge(mm.sem, mm_tb2[pe][pa])
                g_.indirect_dma_start(
                    gtile[:], None, dd["xg_all"].ap(),
                    bass.IndirectOffsetOnAxis(ap=tokix[:, e * 5 + a:e * 5 + a + 1],
                                              axis=0)).then_inc(idm.sem, 16)
                g_.wait_ge(idm.sem, 16 * NTC + 16 * (e * 5 + a + 1))
                g_.drain().then_inc(gp.sem, 1)
            g_.wait_ge(mm.sem, mm_g[e][DC - 1])
            g_.dma_start(wslot[:], wd["wu3"][e].rearrange("(k p) f -> p k f", p=128),
                         max_dma_last_dim=8192).then_inc(ldw.sem, 16)
            g_.wait_ge(mm.sem, mm_u[e][DC - 1])
            g_.dma_start(wslot[:], wd["wd3"][e].rearrange("(k p) f -> p k f", p=128),
                         max_dma_last_dim=8192).then_inc(ldw.sem, 16)
            if e + 1 < EL:
                g_.wait_ge(mm.sem, mm_yy[e][DC - 1])
                g_.dma_start(wslot[:],
                             wd["wg3"][e + 1].rearrange("(k p) f -> p k f", p=128),
                             max_dma_last_dim=8192).then_inc(ldw.sem, 16)
            for a in range(5):
                g_.wait_ge(ve.sem, ve_ytm[e][a])
                g_.dma_start(dd["y_my"].ap()[e * C + a * 128: e * C + (a + 1) * 128, :],
                             ytm[:]).then_inc(lde.sem, 16)
        g_.wait_ge(lde.sem, lde_n[4] + 16 * EL * 5)
        if LVL < 45:
            return
        g_.collective_compute("AllGather", ALU.bypass,
                              replica_groups=[list(range(8))],
                              ins=[dd["y_my"].ap()],
                              outs=[dd["y_all"].ap()]).then_inc(cc.sem)
        g_.wait_ge(cc.sem, cc_y)
        if LVL < 47:
            return
        if LVL == 47:
            g_.indirect_dma_start(
                gtile[:], None, dd["y_all"].ap(),
                bass.IndirectOffsetOnAxis(ap=sidx2[:, 0:1], axis=0)
                ).then_inc(idm.sem, 16)
            g_.wait_ge(idm.sem, 16 * NTC + 16 * EL * 5 + 16)
            return
        for tc in range(NTC):
            if tc > 0:
                g_.wait_ge(ve.sem, ve_z[tc - 1])
            g_.indirect_dma_start(
                gtile[:], None, dd["y_all"].ap(),
                bass.IndirectOffsetOnAxis(ap=sidx2[:, tc:tc + 1], axis=0)
                ).then_inc(idm.sem, 16)
            g_.wait_ge(idm.sem, 16 * NTC + 16 * EL * 5 + 16 * (tc + 1))
            g_.drain().then_inc(gp.sem, 1)
    p.add("gpsimd", g_fn)

    TPB = 4096
    OHB = 5120
    LGO = 1536
    CTO = 2560
    OFO = 3072

    def t_fn(t):
        wait_all(t, ones_waits)
        wait_all(t, ident_waits)
        wait_all(t, x_waits)
        t.wait_ge(ld.sem, ld_c)
        t.wait_ge(gp.sem, gp_c)
        ps_bf = ps[:].bitcast(BF16)
        for tb in range(NTB):
            t.wait_ge(sc.sem, sc_sq1[tb])
            if tb > 0:
                t.wait_ge(ve.sem, ve_v1[tb - 1])
            for i in range(DC):
                csl = slice(i * 512, (i + 1) * 512)
                t.matmul(ps[:, 0:512], ones[:], bf1[:, csl],
                         start=(i == 0), stop=(i == DC - 1))
            for i in range(DC):
                csl = slice(i * 512, (i + 1) * 512)
                inst = t.matmul(ps[:, 512:1024], ones[:], sq1[:, csl],
                                start=(i == 0), stop=(i == DC - 1))
            inst.then_inc(mm.sem, 1)
            t.wait_ge(ve.sem, ve_x1[tb])
            for j in range(4):
                gtc = tb * 4 + j
                if not (tb == 0 and j == 0):
                    ptb, pj = (tb, j - 1) if j > 0 else (tb - 1, 3)
                    t.wait_ge(ve.sem, ve_tm[ptb][pj])
                for c2 in range(DC):
                    inst = t.transpose(
                        ps_bf[:, TPB + c2 * 128:TPB + (c2 + 1) * 128],
                        xln_tb[:, c2 * 512 + j * 128: c2 * 512 + (j + 1) * 128],
                        ident[:])
                inst.then_inc(mm.sem, 1)
                if not (tb == 0 and j == 0):
                    ptb, pj = (tb, j - 1) if j > 0 else (tb - 1, 3)
                    t.wait_ge(ve.sem, ve_rt[ptb][pj])
                for i in range(DC):
                    inst = t.matmul(ps[:, LGO:LGO + 32],
                                    xln_tb[:, i * 512 + j * 128: i * 512 + (j + 1) * 128],
                                    wrt[:, i * 32:(i + 1) * 32],
                                    start=(i == 0), stop=(i == DC - 1))
                inst.then_inc(mm.sem, 1)
                t.wait_ge(ve.sem, ve_rt[tb][j])
                if not (tb == 0 and j == 0):
                    ptb, pj = (tb, j - 1) if j > 0 else (tb - 1, 3)
                    t.wait_ge(ve.sem, ve_ohT[ptb][pj])
                t.transpose(ps_bf[0:32, OHB:OHB + 128],
                            oh_sv[:, gtc * 32:(gtc + 1) * 32],
                            ident[:]).then_inc(mm.sem, 1)
        if LVL < 3:
            return
        t.wait_ge(ve.sem, ve_off1)
        t.transpose(ps[0:1, OFO:OFO + 32], offs[:], ident32[0:32, 0:32]
                    ).then_inc(mm.sem, 1)
        t.wait_ge(ve.sem, ve_offR)
        t.matmul(ps[:, OFO + 32:OFO + 64], ones32[:], offsR[:],
                 start=True, stop=True).then_inc(mm.sem, 1)
        for tc in range(NTC):
            if tc > 0:
                t.wait_ge(ve.sem, ve_slot[tc - 1])
            t.transpose(ps[:, CTO:CTO + 32],
                        cums[:, tc * 128:(tc + 1) * 128],
                        ident32[0:32, 0:32]).then_inc(mm.sem, 1)
        if LVL < 4:
            return
        for e in range(EL):
            for a in range(5):
                t.wait_ge(gp.sem, gp_gth[e][a])
                if e == 0 and a == 0:
                    t.wait_ge(ve.sem, ve_tm[NTB - 1][3])
                elif a > 0:
                    t.wait_ge(ve.sem, ve_bt[e][a - 1])
                else:
                    t.wait_ge(ve.sem, ve_ytm[e - 1][4])
                for c2 in range(DC):
                    inst = t.transpose(
                        ps_bf[:, TPB + c2 * 128:TPB + (c2 + 1) * 128],
                        gtile[:, c2 * 128:(c2 + 1) * 128], ident[:])
                inst.then_inc(mm.sem, 1)
            t.wait_ge(ldw.sem, ld_we[e][0])
            t.wait_ge(ve.sem, ve_bt[e][4])
            for mf in range(DC):
                reg = [0, 1024][mf % 2]
                if mf >= 2:
                    t.wait_ge(sc.sem, sc_sil[e][mf - 2])
                elif e > 0:
                    t.wait_ge(sc.sem, sc_sil[e - 1][DC - 2 + mf])
                for c2 in range(DC):
                    t.matmul(ps[:, reg:reg + 512],
                             wslot[:, c2 * 1024 + mf * 128: c2 * 1024 + (mf + 1) * 128],
                             bufT[:, c2 * C: c2 * C + 512],
                             start=(c2 == 0), stop=(c2 == DC - 1))
                for c2 in range(DC):
                    inst = t.matmul(ps[:, reg + 512:reg + 640],
                                    wslot[:, c2 * 1024 + mf * 128: c2 * 1024 + (mf + 1) * 128],
                                    bufT[:, c2 * C + 512: (c2 + 1) * C],
                                    start=(c2 == 0), stop=(c2 == DC - 1))
                inst.then_inc(mm.sem, 1)
            t.wait_ge(ldw.sem, ld_we[e][1])
            for mf in range(DC):
                reg = [0, 1024][mf % 2]
                if mf >= 2:
                    t.wait_ge(ve.sem, ve_gu[e][mf - 2])
                else:
                    t.wait_ge(sc.sem, sc_sil[e][DC - 1])
                for c2 in range(DC):
                    t.matmul(ps[:, reg:reg + 512],
                             wslot[:, c2 * 1024 + mf * 128: c2 * 1024 + (mf + 1) * 128],
                             bufT[:, c2 * C: c2 * C + 512],
                             start=(c2 == 0), stop=(c2 == DC - 1))
                for c2 in range(DC):
                    inst = t.matmul(ps[:, reg + 512:reg + 640],
                                    wslot[:, c2 * 1024 + mf * 128: c2 * 1024 + (mf + 1) * 128],
                                    bufT[:, c2 * C + 512: (c2 + 1) * C],
                                    start=(c2 == 0), stop=(c2 == DC - 1))
                inst.then_inc(mm.sem, 1)
            t.wait_ge(ldw.sem, ld_we[e][2])
            for mf in range(DC):
                reg = [0, 1024][mf % 2]
                if mf >= 2:
                    t.wait_ge(ve.sem, ve_ysb[e][mf - 2])
                else:
                    t.wait_ge(ve.sem, ve_gu[e][DC - 1])
                for c2 in range(DC):
                    t.matmul(ps[:, reg:reg + 512],
                             wslot[:, c2 * 1024 + mf * 128: c2 * 1024 + (mf + 1) * 128],
                             usb[:, c2 * C: c2 * C + 512],
                             start=(c2 == 0), stop=(c2 == DC - 1))
                for c2 in range(DC):
                    inst = t.matmul(ps[:, reg + 512:reg + 640],
                                    wslot[:, c2 * 1024 + mf * 128: c2 * 1024 + (mf + 1) * 128],
                                    usb[:, c2 * C + 512: (c2 + 1) * C],
                                    start=(c2 == 0), stop=(c2 == DC - 1))
                inst.then_inc(mm.sem, 1)
            for a in range(5):
                t.wait_ge(ve.sem, ve_ysb[e][DC - 1])
                if a == 0:
                    t.wait_ge(ve.sem, ve_bt[e][4])
                else:
                    t.wait_ge(ve.sem, ve_ytm[e][a - 1])
                for c2 in range(DC):
                    inst = t.transpose(
                        ps_bf[:, TPB + c2 * 128:TPB + (c2 + 1) * 128],
                        gsb[:, c2 * C + a * 128: c2 * C + (a + 1) * 128],
                        ident[:])
                inst.then_inc(mm.sem, 1)
        if LVL < 50:
            return
        for tc in range(NTC):
            t.wait_ge(ve.sem, ve_z[tc])
            if tc == 0:
                t.wait_ge(ve.sem, ve_ytm[EL - 1][4])
            else:
                t.wait_ge(ve.sem, ve_fin[tc - 1][DC - 1])
            for c2 in range(DC):
                t.transpose(ps_bf[:, TPB + c2 * 128:TPB + (c2 + 1) * 128],
                            ztile[:, c2 * 128:(c2 + 1) * 128],
                            ident[:]).then_inc(mm.sem, 1)
    p.add("tensor", t_fn)

    ve_rt_pre = [[0] * 4 for _ in range(NTB)]
    ve.v = 0
    for tb in range(NTB):
        ve_v1[tb] = ve.inc()
        ve_x1[tb] = ve.inc()
        for j in range(4):
            ve_tm[tb][j] = ve.inc()
            ve_rt_pre[tb][j] = ve.inc()
            ve_rt[tb][j] = ve.inc()
            ve_ohT[tb][j] = ve.inc()
    ve_scan = ve.inc()
    ve_off1 = ve.inc()
    ve_offR = ve.inc()
    ve_offB = ve.inc()
    for tc in range(NTC):
        ve_slot[tc] = ve.inc()
    ve_tok = ve.inc()
    for e in range(EL):
        for a in range(5):
            ve_bt[e][a] = ve.inc()
        for mf in range(DC):
            ve_gu[e][mf] = ve.inc()
        for mf in range(DC):
            ve_ysb[e][mf] = ve.inc()
        for a in range(5):
            ve_ytm[e][a] = ve.inc()
    for tc in range(NTC):
        ve_z[tc] = ve.inc()
        for c2 in range(DC):
            ve_fin[tc][c2] = ve.inc()

    tokh = p.sbuf(tag + "_tokh", [128, 20], F32)
    otmp = p.sbuf(tag + "_otmp", [E1, 8], F32)

    def s_fn(s):
        wait_all(s, x_waits)
        for tb in range(NTB):
            sl = slice(tb * 512, (tb + 1) * 512)
            if tb > 0:
                s.wait_ge(ve.sem, ve_x1[tb - 1])
                s.wait_ge(mm.sem, mm_tp[tb - 1][3])
                s.wait_ge(mm.sem, mm_lg[tb - 1][3])
            for i in range(DC):
                csl = slice(i * 512, (i + 1) * 512)
                s.activation(sq1[:, csl], xt[i][:, sl], AF.Square)
                s.activation(bf1[:, csl], xt[i][:, sl], AF.Copy)
            s.drain().then_inc(sc.sem, 1)
            s.wait_ge(ve.sem, ve_v1[tb])
            s.activation(rstd, var, AF.Ln, bias=p.cb[:, 0:1])
            s.drain()
            s.activation(rstd, rstd, AF.Exp, scale=-0.5)
            s.drain().then_inc(sc.sem, 1)
            for j in range(4):
                s.wait_ge(mm.sem, mm_lg[tb][j])
                s.wait_ge(ve.sem, ve_rt_pre[tb][j])
                s.activation(r_e[:], ps[:, LGO:LGO + 32], AF.Exp,
                             bias=r_nm[:, 0:1], accum_out=r_s[:])
                s.drain().then_inc(sc.sem, 1)
        if LVL < 4:
            return
        for e in range(EL):
            for mf in range(DC):
                reg = [0, 1024][mf % 2]
                s.wait_ge(mm.sem, mm_g[e][mf])
                s.activation(gsb[:, mf * C:(mf + 1) * C], ps[:, reg:reg + 640],
                             AF.Silu)
                s.drain().then_inc(sc.sem, 1)
    p.add("scalar", s_fn)

    def v_fn(v):
        ps_bf = ps[:].bitcast(BF16)
        for tb in range(NTB):
            sl = slice(tb * 512, (tb + 1) * 512)
            v.wait_ge(mm.sem, mm_s1[tb])
            v.tensor_scalar(mu, ps[:, 0:512], 1.0 / D, None, op0=ALU.mult)
            v.tensor_scalar(msq, ps[:, 512:1024], 1.0 / D, None, op0=ALU.mult)
            v.drain()
            v.tensor_tensor(tmp, mu, mu, op=ALU.mult)
            v.drain()
            v.tensor_tensor(var, msq, tmp, op=ALU.subtract)
            v.drain().then_inc(ve.sem, 1)
            v.wait_ge(sc.sem, sc_r1[tb])
            v.tensor_tensor(mur, mu, rstd, op=ALU.mult)
            v.drain()
            for i in range(DC):
                csl = slice(i * 512, (i + 1) * 512)
                v.tensor_tensor(u, xt[i][:, sl], rstd, op=ALU.mult)
                v.drain()
                v.tensor_tensor(xln_tb[:, csl], u, mur, op=ALU.subtract)
                v.drain()
            v.nop().then_inc(ve.sem, 1)
            for j in range(4):
                gtc = tb * 4 + j
                v.wait_ge(mm.sem, mm_tp[tb][j])
                v.tensor_copy(xln_tm[:, gtc * 1024:(gtc + 1) * 1024],
                              ps_bf[:, TPB:TPB + 1024])
                v.drain().then_inc(ve.sem, 1)
                v.wait_ge(mm.sem, mm_lg[tb][j])
                v.reduce_max(r_m[:], ps[:, LGO:LGO + 32], axis=AX)
                v.drain()
                v.tensor_scalar(r_nm[:], r_m[:], -1.0, None, op0=ALU.mult)
                v.drain().then_inc(ve.sem, 1)
                v.wait_ge(sc.sem, sc_exp[tb][j])
                v.reciprocal(gate[:, gtc:gtc + 1], r_s[:])
                v.drain()
                v.tensor_scalar(oh_sv[:, gtc * 32:(gtc + 1) * 32],
                                ps[:, LGO:LGO + 32], r_m[:, 0:1], None,
                                op0=ALU.is_equal)
                v.tensor_scalar(r_oh[:], ps[:, LGO:LGO + 32], r_m[:, 0:1], None,
                                op0=ALU.is_equal)
                v.drain()
                v.scalar_tensor_tensor(r_e[:], r_oh[:], 1.0, iotaE[:],
                                       op0=ALU.mult, op1=ALU.mult,
                                       accum_out=eidx[:, gtc:gtc + 1])
                v.drain().then_inc(ve.sem, 1)
                v.wait_ge(mm.sem, mm_ohT[tb][j])
                v.tensor_copy(ohT[:, gtc * 128:(gtc + 1) * 128],
                              ps_bf[0:32, OHB:OHB + 128])
                v.drain().then_inc(ve.sem, 1)
        if LVL < 2:
            return
        v.wait_ge(gp.sem, gp_c)
        v.tensor_tensor_scan(cums[:], sones[:], ohT[:], 0.0,
                             op0=ALU.mult, op1=ALU.add)
        v.drain().then_inc(ve.sem, 1)
        if LVL < 3:
            return
        v.wait_ge(lde.sem, lde_n[3])
        v.scalar_tensor_tensor(otmp[:], tot_sb[:], 1.0, rmask[:],
                               op0=ALU.mult, op1=ALU.mult, accum_out=offs[:])
        v.drain().then_inc(ve.sem, 1)
        v.wait_ge(mm.sem, mm_toff)
        v.tensor_copy(offsR[:], ps[0:1, OFO:OFO + 32])
        v.drain().then_inc(ve.sem, 1)
        v.wait_ge(mm.sem, mm_boff)
        v.tensor_copy(offsB[:], ps[:, OFO + 32:OFO + 64])
        v.drain().then_inc(ve.sem, 1)
        for tc in range(NTC):
            osl = slice(tc * 32, (tc + 1) * 32)
            v.wait_ge(mm.sem, mm_ctm[tc])
            v.tensor_copy(cumtm[:], ps[:, CTO:CTO + 32])
            v.drain()
            v.scalar_tensor_tensor(r_e[:], oh_sv[:, osl], 1.0, cumtm[:],
                                   op0=ALU.mult, op1=ALU.mult,
                                   accum_out=r_t1[:])
            v.drain()
            v.scalar_tensor_tensor(r_e[:], oh_sv[:, osl], 1.0, offsB[:],
                                   op0=ALU.mult, op1=ALU.mult,
                                   accum_out=r_t2[:])
            v.drain()
            v.scalar_tensor_tensor(r_t3[:], r_t1[:], -1.0, r_t2[:],
                                   op0=ALU.add, op1=ALU.add)
            v.drain()
            v.tensor_scalar(r_t4[:], r_t3[:], float(C1), None, op0=ALU.is_lt)
            v.drain()
            v.tensor_tensor(gatef[:, tc:tc + 1], gate[:, tc:tc + 1], r_t4[:],
                            op=ALU.mult)
            v.drain()
            v.scalar_tensor_tensor(r_t1[:], eidx[:, tc:tc + 1], float(C1),
                                   r_t3[:], op0=ALU.mult, op1=ALU.add)
            v.drain()
            v.tensor_scalar(r_t2[:], r_t4[:], -1.0, 1.0, op0=ALU.mult,
                            op1=ALU.add)
            v.drain()
            v.tensor_tensor(r_t1[:], r_t1[:], r_t4[:], op=ALU.mult)
            v.drain()
            v.tensor_tensor(r_t5[:], r_t2[:], iotap[:], op=ALU.mult)
            v.drain()
            v.tensor_tensor(r_t5[:], r_t5[:], r_t1[:], op=ALU.add)
            v.drain()
            v.tensor_copy(sidx2[:, tc:tc + 1], r_t5[:])
            v.drain()
            v.scalar_tensor_tensor(r_t5[:], r_t2[:], 20480.0, r_t5[:],
                                   op0=ALU.mult, op1=ALU.add)
            v.drain()
            v.tensor_copy(sidx[:, tc:tc + 1], r_t5[:])
            v.drain().then_inc(ve.sem, 1)
        v.wait_ge(gp.sem, gp_xz)
        v.tensor_scalar(tokf[:], toksb[:], -1.0, None, op0=ALU.add)
        v.drain()
        v.tensor_scalar(tokh[:], tokf[:], 0.0, None, op0=ALU.max)
        v.drain()
        v.tensor_copy(tokix[:], tokh[:])
        v.drain().then_inc(ve.sem, 1)
        if LVL < 4:
            return
        for e in range(EL):
            for a in range(5):
                v.wait_ge(mm.sem, mm_tb2[e][a])
                v.tensor_copy(
                    bufT[:].rearrange("p (c s) -> p c s", s=C)[:, :, a * 128:(a + 1) * 128],
                    ps_bf[:, TPB:TPB + 1024].rearrange("p (c s) -> p c s", s=128))
                v.drain().then_inc(ve.sem, 1)
            for mf in range(DC):
                reg = [0, 1024][mf % 2]
                v.wait_ge(mm.sem, mm_u[e][mf])
                v.wait_ge(sc.sem, sc_sil[e][mf])
                v.tensor_tensor(usb[:, mf * C:(mf + 1) * C],
                                ps[:, reg:reg + 640],
                                gsb[:, mf * C:(mf + 1) * C], op=ALU.mult)
                v.drain().then_inc(ve.sem, 1)
            for mf in range(DC):
                reg = [0, 1024][mf % 2]
                v.wait_ge(mm.sem, mm_yy[e][mf])
                v.tensor_copy(gsb[:, mf * C:(mf + 1) * C], ps[:, reg:reg + 640])
                v.drain().then_inc(ve.sem, 1)
            for a in range(5):
                v.wait_ge(mm.sem, mm_ty[e][a])
                if not (e == 0 and a == 0):
                    pe, pa = (e, a - 1) if a > 0 else (e - 1, 4)
                    v.wait_ge(lde.sem, lde_n[4] + 16 * (pe * 5 + pa + 1))
                v.tensor_copy(ytm[:], ps_bf[:, TPB:TPB + 1024])
                v.drain().then_inc(ve.sem, 1)
        if LVL < 50:
            return
        for tc in range(NTC):
            v.wait_ge(gp.sem, gp_yg[tc])
            v.scalar_tensor_tensor(ztile, gtile[:], gatef[:, tc:tc + 1],
                                   xln_tm[:, tc * 1024:(tc + 1) * 1024],
                                   op0=ALU.mult, op1=ALU.add)
            v.drain().then_inc(ve.sem, 1)
            v.wait_ge(mm.sem, mm_tz[tc][DC - 1])
            for c2 in range(DC):
                v.scalar_tensor_tensor(
                    xt[c2][:, tc * 128:(tc + 1) * 128],
                    ps_bf[:, TPB + c2 * 128:TPB + (c2 + 1) * 128], 0.5,
                    xt[c2][:, tc * 128:(tc + 1) * 128],
                    op0=ALU.mult, op1=ALU.add)
                v.drain().then_inc(ve.sem, 1)
    p.add("vector", v_fn)

    if LVL >= 50:
        return [(ve, ve_fin[NTC - 1][DC - 1])]
    if LVL >= 4:
        return [(ve, ve_ytm[EL - 1][4])]
    if LVL >= 3:
        return [(ve, ve_tok)]
    if LVL >= 2:
        return [(ve, ve_scan)]
    return [(ve, ve_ohT[NTB - 1][3])]


def build_l1ffn_launch():
    nc = bass.Bass("TRN2", target_bir_lowering=False)
    x_d = nc.dram_tensor("x", [D, T], F32, kind="ExternalInput").ap()
    shapes = {"wrt": [1024, 32], "wg3": [4, 1024, 1024], "wu3": [4, 1024, 1024],
              "wd3": [4, 1024, 1024], "rmask": [E1, 8], "ids": [128, 16],
              "myoff": [1, 1], "ident32": [128, 128]}
    dts = {"ids": I32, "myoff": I32}
    wd = {n: nc.dram_tensor(n, s, dts.get(n, F32), kind="ExternalInput").ap()
          for n, s in shapes.items()}
    ident_d = nc.dram_tensor("ident", [128, 128], F32, kind="ExternalInput").ap()
    xo_d = nc.dram_tensor("xo", [D, T], F32, kind="ExternalOutput").ap()
    dd = {
        "xg_my": nc.dram_tensor("xg_my", [T, 1024], BF16, kind="Internal"),
        "xg_all": nc.dram_tensor("xg_all", [8 * T, 1024], BF16, kind="Internal"),
        "tot_my": nc.dram_tensor("tot_my", [E1, 1], F32, kind="Internal"),
        "tot_all": nc.dram_tensor("tot_all", [8 * E1, 1], F32, kind="Internal"),
        "tbl_my": nc.dram_tensor("tbl_my", [20608, 1], I32, kind="Internal"),
        "tblr": nc.dram_tensor("tblr", [20608, 1], I32, kind="Internal"),
        "y_my": nc.dram_tensor("y_my", [2560, 1024], BF16, kind="Internal"),
        "y_all": nc.dram_tensor("y_all", [20480, 1024], BF16, kind="Internal"),
    }
    with contextlib.ExitStack() as stack:
        p = Prog(nc, stack.enter_context)
        xt = [p.sbuf(f"xt{i}", [128, T], F32) for i in range(DC)]
        ps = p.psum("ps", [128, 4096])
        scr_t = p.sbuf("scr", [128, 24 * 512], BF16)
        scr = scr_t[:]
        ones, ones_w, ident, ident_w, _ = emit_consts(p, ident_d)
        xw = emit_load_xt(p, xt, x_d)
        outw = emit_l1ffn(p, "c1", xt, dd, wd, xw, ones, ones_w, ps, scr,
                          ident, ident_w)
        emit_store_xt(p, xo_d, xt, outw)
        p.build_block()
    return nc



def _rope_tables():
    inv_freq = 1.0 / 10000.0 ** (np.arange(0, 64, 2, dtype=np.float32) / 64)
    t = np.arange(T, dtype=np.float32)
    freqs = np.outer(t, inv_freq)
    cos = np.concatenate([np.cos(freqs), np.cos(freqs)], -1)[:, ::2]
    sin = np.concatenate([np.sin(freqs), np.sin(freqs)], -1)[:, ::2]
    return (np.ascontiguousarray(np.tile(cos, (1, 4))).astype(np.float32),
            np.ascontiguousarray(np.tile(sin, (1, 4))).astype(np.float32))


def _mask4():
    m = np.zeros((128, 4 * 512), np.float32)
    for r in range(4):
        for pp in range(128):
            m[pp, r * 512:(r + 1) * 512] = np.where(
                np.arange(512) >= r * 128 + pp, 0.0, -2000.0)
    return m


_NC_CACHE = {}


def _get_nc(name):
    if name not in _NC_CACHE:
        _NC_CACHE[name] = {
            "l0ffn": build_l0ffn_launch,
            "l0attn": build_l0attn_launch,
            "l1ffn": build_l1ffn_launch,
            "l1attn": build_l1attn_launch,
        }[name]()
    return _NC_CACHE[name]


def build_fused():
    nc = bass.Bass("TRN2", target_bir_lowering=False)
    x_d = nc.dram_tensor("x", [D, T], F32, kind="ExternalInput").ap()
    xo_d = nc.dram_tensor("xo", [D, T], F32, kind="ExternalOutput").ap()
    ident_d = nc.dram_tensor("ident", [128, 128], F32, kind="ExternalInput").ap()
    sel8_d = nc.dram_tensor("sel8", [E0, E0 * 128], F32, kind="ExternalInput").ap()
    wg_d = nc.dram_tensor("wg", [D, E0], F32, kind="ExternalInput").ap()
    w1_d = nc.dram_tensor("w1", [E0, D, H0], F32, kind="ExternalInput").ap()
    w2_d = nc.dram_tensor("w2", [E0, H0, D], F32, kind="ExternalInput").ap()
    a0_shapes = {"wuqT": [512, 1024], "wdq": [512, 1024], "wdqT": [1024, 512],
                 "wdkvT": [1024, 512], "wuk": [1024, 512], "wuv": [1024, 512],
                 "woT": [1024, 1024], "mask4": [128, 4 * 512]}
    wd_a0 = {n: nc.dram_tensor("a0_" + n, s, F32, kind="ExternalInput").ap()
             for n, s in a0_shapes.items()}
    b1_shapes = {"wdqT": [1024, 512], "wdkvT": [1024, 512], "wuqT": [512, 1024],
                 "wukT": [512, 1024], "wuvT": [512, 1024], "woutT": [1024, 1024],
                 "lnsc": [128, DC], "lnbs": [128, DC],
                 "cosR": [2048, 128], "sinR": [2048, 128]}
    wd_b1 = {n: nc.dram_tensor("b1_" + n, s, F32, kind="ExternalInput").ap()
             for n, s in b1_shapes.items()}
    c1_shapes = {"wrt": [1024, 32], "wg3": [4, 1024, 1024],
                 "wu3": [4, 1024, 1024], "wd3": [4, 1024, 1024],
                 "rmask": [E1, 8], "ids": [128, 16], "myoff": [1, 1],
                 "ident32": [128, 128]}
    c1_dts = {"ids": I32, "myoff": I32}
    wd_c1 = {n: nc.dram_tensor("c1_" + n, s, c1_dts.get(n, F32),
                               kind="ExternalInput").ap()
             for n, s in c1_shapes.items()}
    dd = {
        "xg_my": nc.dram_tensor("xg_my", [T, 1024], BF16, kind="Internal"),
        "xg_all": nc.dram_tensor("xg_all", [8 * T, 1024], BF16, kind="Internal"),
        "tot_my": nc.dram_tensor("tot_my", [E1, 1], F32, kind="Internal"),
        "tot_all": nc.dram_tensor("tot_all", [8 * E1, 1], F32, kind="Internal"),
        "tbl_my": nc.dram_tensor("tbl_my", [20608, 1], I32, kind="Internal"),
        "tblr": nc.dram_tensor("tblr", [20608, 1], I32, kind="Internal"),
        "y_my": nc.dram_tensor("y_my", [2560, 1024], BF16, kind="Internal"),
        "y_all": nc.dram_tensor("y_all", [20480, 1024], BF16, kind="Internal"),
    }
    ARENA_BYTES = 116 * 1024
    with contextlib.ExitStack() as stack:
        p = Prog(nc, stack.enter_context)
        xt = [p.sbuf(f"xt{i}", [128, T], F32) for i in range(DC)]
        ps = p.psum("ps", [128, 4096])
        scr_t = p.sbuf("scr", [128, 24 * 512], BF16)
        scr = scr_t[:]
        arena_t = p.en(nc.sbuf_tensor("arena", [128, ARENA_BYTES // 2], BF16))
        ones, ones_w, ident, ident_w, sel8 = emit_consts(p, ident_d, sel8_d)
        xw = emit_load_xt(p, xt, x_d)
        p.set_arena(arena_t, ARENA_BYTES)

        def stage_ln_ffn(tagln, tagf, gate):
            p.begin_stage(gate)
            xln = [p.sbuf(f"{tagln}_x{i}", [128, T], BF16) for i in range(DC)]
            lnw = emit_ln(p, tagln, xt, xln, gate, ones=ones,
                          ones_waits=ones_w + ident_w, ps=ps, scr=scr)
            return emit_l0ffn(p, tagf, xt, xln, lnw, wg_d, w1_d, w2_d,
                              ones, ones_w, ident, ident_w, ps, scr,
                              sel8, ident_w)

        o1 = stage_ln_ffn("ln0a", "f0a", xw)
        p.begin_stage(o1)
        xln2 = [p.sbuf(f"ln0b_x{i}", [128, T], BF16) for i in range(DC)]
        lnw2 = emit_ln(p, "ln0b", xt, xln2, o1, ones=ones,
                       ones_waits=ones_w + ident_w, ps=ps, scr=scr)
        ln_mm = p.ctr["ln0b_mm"]
        o2 = emit_l0attn(p, "a0", xt, xln2, lnw2, wd_a0, ones, ones_w, ps, scr,
                         [(ln_mm, ln_mm.v)], ident, ident_w)
        o3 = stage_ln_ffn("ln0c", "f0b", o2)
        p.begin_stage(o3)
        o4 = emit_l1ffn(p, "c1a", xt, dd, wd_c1, o3, ones, ones_w, ps, scr,
                        ident, ident_w)
        p.begin_stage(o4)
        o5 = emit_l1attn(p, "b1", xt, wd_b1, o4, ones, ones_w, ps, scr,
                         ident, ident_w)
        p.begin_stage(o5)
        o6 = emit_l1ffn(p, "c1b", xt, dd, wd_c1, o5, ones, ones_w, ps, scr,
                        ident, ident_w)
        p.begin_stage(o6)
        emit_store_xt(p, xo_d, xt, o6)
        p.build_block()
    return nc


FUSED = True


def kernel(**inputs):
    f32 = np.float32
    eye = np.eye(128, dtype=f32)
    cosR, sinR = _rope_tables()
    mask4 = _mask4()
    sel8 = np.repeat(np.eye(E0, dtype=f32), 128, axis=1)
    tr = lambda a: np.ascontiguousarray(np.asarray(a, dtype=f32).T)
    asf = lambda a: np.ascontiguousarray(np.asarray(a, dtype=f32))

    x = np.asarray(inputs["L0_attn_x"], dtype=f32)
    xT = [tr(x[c]) for c in range(NCORES)]

    m_l0ffn = {
        "wg": asf(inputs["L0_ffn_Wg"]), "w1": asf(inputs["L0_ffn_W1"]),
        "w2": asf(inputs["L0_ffn_W2"]), "ident": eye, "sel8": sel8,
    }
    m_l0attn = {
        "wuqT": tr(inputs["L0_attn_W_uq"]), "wdq": asf(inputs["L0_attn_W_dq"]),
        "wdqT": tr(inputs["L0_attn_W_dq"]), "wdkvT": tr(inputs["L0_attn_W_dkv"]),
        "wuk": asf(inputs["L0_attn_W_uk"]), "wuv": asf(inputs["L0_attn_W_uv"]),
        "woT": tr(inputs["L0_attn_W_o"]), "mask4": mask4, "ident": eye,
    }
    g = np.asarray(inputs["L1_attn_ln_scale"], dtype=f32)
    b = np.asarray(inputs["L1_attn_ln_bias"], dtype=f32)
    m_l1attn = {
        "wdqT": tr(inputs["L1_attn_w_down_q"]),
        "wdkvT": tr(inputs["L1_attn_w_down_kv"]),
        "wuqT": tr(inputs["L1_attn_w_up_q"]), "wukT": tr(inputs["L1_attn_w_up_k"]),
        "wuvT": tr(inputs["L1_attn_w_up_v"]), "woutT": tr(inputs["L1_attn_w_out"]),
        "lnsc": np.ascontiguousarray(g.reshape(DC, 128).T),
        "lnbs": np.ascontiguousarray(b.reshape(DC, 128).T),
        "cosR": cosR, "sinR": sinR, "ident": eye,
    }
    wg_all = asf(inputs["L1_ffn_w_gate"])
    wu_all = asf(inputs["L1_ffn_w_up"])
    wdn_all = asf(inputs["L1_ffn_w_down"])

    def l1ffn_map(c):
        rmask = np.zeros((E1, 8), f32)
        rmask[:, :c] = 1.0
        ids = (c * T + np.arange(128)[:, None] + 128 * np.arange(16)[None, :]
               + 1).astype(np.int32)
        return {
            "wrt": asf(inputs["L1_ffn_w_router"]),
            "wg3": wg_all[4 * c:4 * c + 4], "wu3": wu_all[4 * c:4 * c + 4],
            "wd3": wdn_all[4 * c:4 * c + 4], "rmask": rmask, "ids": ids,
            "myoff": np.array([[c * 2560]], np.int32), "ident32": eye,
        }

    if FUSED:
        if "fused" not in _NC_CACHE:
            _NC_CACHE["fused"] = build_fused()
        nc = _NC_CACHE["fused"]
        in_maps = []
        for c in range(NCORES):
            m = {"x": xT[c], "ident": eye, "sel8": sel8,
                 "wg": m_l0ffn["wg"], "w1": m_l0ffn["w1"], "w2": m_l0ffn["w2"]}
            for k, v in m_l0attn.items():
                if k != "ident":
                    m["a0_" + k] = v
            for k, v in m_l1attn.items():
                if k != "ident":
                    m["b1_" + k] = v
            for k, v in l1ffn_map(c).items():
                m["c1_" + k] = v
            in_maps.append(m)
        res = run_bass_kernel_spmd(nc, in_maps, core_ids=list(range(NCORES)))
        out = np.stack([res.results[c]["xo"].T for c in range(NCORES)], axis=0)
        return np.ascontiguousarray(out, dtype=np.float32)

    m_l1ffn = [l1ffn_map(c) for c in range(NCORES)]

    def launch(name, shared_map, per_core_map=None):
        nonlocal xT
        nc = _get_nc(name)
        in_maps = []
        for c in range(NCORES):
            m = dict(shared_map if per_core_map is None else per_core_map[c])
            m["x"] = xT[c]
            in_maps.append(m)
        res = run_bass_kernel_spmd(nc, in_maps, core_ids=list(range(NCORES)))
        xT = [np.ascontiguousarray(res.results[c]["xo"]) for c in range(NCORES)]

    launch("l0ffn", m_l0ffn)
    launch("l0attn", m_l0attn)
    launch("l0ffn", m_l0ffn)
    launch("l1ffn", None, m_l1ffn)
    launch("l1attn", m_l1attn)
    launch("l1ffn", None, m_l1ffn)
    out = np.stack([xT[c].T for c in range(NCORES)], axis=0)
    return np.ascontiguousarray(out, dtype=np.float32)
